# version 10
# baseline (speedup 1.0000x reference)
import numpy as np
import ml_dtypes
import concourse.bass as bass
import concourse.mybir as mybir
from concourse.bass_utils import run_bass_kernel_spmd

F32 = mybir.dt.float32
F16 = mybir.dt.float16
BF16 = mybir.dt.bfloat16
I32 = mybir.dt.int32
ALU = mybir.AluOpType
AFT = mybir.ActivationFunctionType
AX = mybir.AxisListType

MAGIC = 12582912.0
INV_QA = 1.0 / 32767.0
EPS_LN = 1e-5
MSLACK = 3.0
ENGS = ("t", "v", "s", "g", "y")
SBUF_BASE = 17408
SBUF_CAP = 229200


class Prog:
    def __init__(self, nc, sem_names):
        self.nc = nc
        self.streams = {e: [] for e in ENGS}
        self.counts = {k: 0 for k in sem_names}
        self.water = {}
        self.handles = {}
        for s in sem_names:
            cm = nc.semaphore(s)
            self.handles[s] = cm.__enter__()

    def op(self, eng, maker, inc=None, n=1):
        if inc is not None:
            self.counts[inc] += n
        h = self.handles[inc] if inc is not None else None

        def thunk(e, maker=maker, h=h, n=n):
            ins = maker(e)
            if h is not None:
                ins.then_inc(h, n)
        self.streams[eng].append(thunk)
        return self.counts[inc] if inc is not None else None

    def dma(self, eng, out, in_, ch="da"):
        self.wait(eng, ch, self.counts[ch])
        return self.op(eng, lambda e, o=out, i=in_: e.dma_start(out=o, in_=i),
                       inc=ch, n=16)

    def wait(self, eng, sem, val=None):
        if val is None:
            val = self.counts[sem]
        if val <= 0:
            return
        key = (eng, sem)
        if self.water.get(key, -1) >= val:
            return
        self.water[key] = val
        h = self.handles[sem]
        self.streams[eng].append(lambda e, h=h, v=val: e.wait_ge(h, v))

    DMA_CHANNELS = ("da", "db", "dc", "dd", "de", "df", "dg")

    def barrier(self):
        for ch in self.DMA_CHANNELS:
            self.wait("y", ch)
        streams, self.streams = self.streams, {e: [] for e in ENGS}
        if not any(streams.values()):
            return
        with self.nc.Block() as block:
            for ename, deco in (("t", block.tensor), ("v", block.vector),
                                ("s", block.scalar), ("g", block.gpsimd),
                                ("y", block.sync)):
                th = streams[ename]
                if th:
                    def body(eng, th=th):
                        for f in th:
                            f(eng)
                    deco(body)
        for k in self.counts:
            for e in ENGS:
                self.water[(e, k)] = self.counts[k]


def build_nc(D, I3, NTOK, n_group):
    NSEQ = NTOK * n_group
    DT = D // 128
    IT = I3 // 128
    TT = NTOK // 128
    TCH = NTOK // 512
    NCH = NTOK // 512
    JT = NSEQ // 128
    JCH = NSEQ // 512
    EH = max(D // 512, 1)
    EW = min(D, 512)
    KVK = DT * 128 * NTOK
    KVV = NTOK * D
    NVT = NSEQ // D
    NV = NVT // n_group
    assert NV >= 1 and NVT * D == NSEQ

    nc = bass.Bass("TRN2", target_bir_lowering=False)

    xq_d = nc.dram_tensor("xq", [NTOK, D], F32, kind="ExternalInput")
    xf_d = nc.dram_tensor("xf", [NTOK, D], F32, kind="ExternalInput")
    wT_d = {n: nc.dram_tensor(n, [128, DT * D], F32, kind="ExternalInput")
            for n in ("wq", "wk", "wv", "wo")}
    wg_d = nc.dram_tensor("wg", [I3, D], F32, kind="ExternalInput")
    wu_d = nc.dram_tensor("wu", [I3, D], F32, kind="ExternalInput")
    wd_d = nc.dram_tensor("wd", [D, I3], F32, kind="ExternalInput")
    idb_d = nc.dram_tensor("idb", [128, 128], BF16, kind="ExternalInput")
    out_d = nc.dram_tensor("out", [NTOK, D], F32, kind="ExternalOutput")

    xnf_h = nc.dram_tensor("xnf_h", [TT, 128, D], F32)
    q_h = nc.dram_tensor("q_h", [128, DT, NTOK], BF16)
    kvin_h = nc.dram_tensor("kvin_h", [KVK + KVV], BF16)
    kvout_h = nc.dram_tensor("kvout_h", [n_group, KVK + KVV], BF16)
    wuT_h = nc.dram_tensor("wuT_h", [128, DT, I3], BF16)
    wgT_h = nc.dram_tensor("wgT_h", [128, DT, I3], BF16)
    wdT_h = nc.dram_tensor("wdT_h", [128, IT, D], BF16)
    up_h = nc.dram_tensor("up_h", [IT, 128, NTOK], F16)
    ga_h = nc.dram_tensor("ga_h", [IT, 128, NTOK], F16)
    p_h = nc.dram_tensor("p_h", [IT, 128, NTOK], F16)
    q2_h = nc.dram_tensor("q2_h", [IT, 128, NTOK], F16)
    o_h = nc.dram_tensor("o_h", [128, IT, NTOK], BF16)
    cc_in = [nc.dram_tensor(f"cc{i}_in", [8], F32) for i in range(4)]
    cc_out = [nc.dram_tensor(f"cc{i}_out", [8], F32) for i in range(4)]
    pb_h = nc.dram_tensor("pb_h", [128], F32)
    sb_h = nc.dram_tensor("sb_h", [64], F32)

    def sba(name, shape, dt, off):
        off = off + SBUF_BASE
        itemsz = {F32: 4, F16: 2, BF16: 2, I32: 4}[dt]
        end = off + shape[-1] * itemsz
        assert end <= SBUF_CAP, f"{name}: ends {end} > {SBUF_CAP}"
        return nc.alloc_sbuf_tensor_at(name, shape, dt, offset=off)

    MG = sba("MG", [128, 1], F32, 0)
    EPSB = sba("EPSB", [128, 1], F32, 32)
    ONESB = sba("ONESB", [128, 1], BF16, 64)
    ONE1B = sba("ONE1B", [1, 128], BF16, 96)
    IDB = sba("IDB", [128, 128], BF16, 352)
    STAT = sba("STAT", [128, 16], F32, 608)
    MAXC = sba("MAXC", [128, 32], F32, 672)
    SCAL = sba("SCAL", [128, 16], F32, 800)
    SC1 = sba("SC1", [1, 32], F32, 864)
    RECS = sba("RECS", [128, 4], F32, 992)
    SUMS = sba("SUMS", [128, 4], F32, 1024)
    MXR = sba("MXR", [1, 128], F32, 1056)
    X2S = sba("X2S", [128, 1024], F32, 2048)
    QAR = sba("QAR", [1, NTOK], BF16, 6144)
    ch0 = 17408
    CW = NTOK // 2
    CIN = sba("CIN", [128, CW], F16, ch0)
    CF1 = sba("CF1", [128, CW], F32, ch0 + 2 * CW)
    CI1 = sba("CI1", [128, CW], I32, ch0 + 6 * CW)
    CF2 = sba("CF2", [128, CW], F32, ch0 + 10 * CW)
    CI2 = sba("CI2", [128, CW], I32, ch0 + 14 * CW)
    CO16 = sba("CO16", [128, CW], F16, ch0 + 18 * CW)
    CIN2 = sba("CIN2", [128, CW], F16, ch0 + 20 * CW)
    OBB = sba("OBB", [128, CW], BF16, ch0 + 10 * CW)
    QCB = sba("QCB", [128, DT * 512], BF16, ch0 + 22 * CW)
    ch_end = ch0 + 22 * NTOK
    XNQB = sba("XNQB", [128, D], BF16, ch0)
    XTL = [sba(f"XTL{i}", [128, D], F32, ch0 + 2 * D + i * 4 * D)
           for i in range(2)]
    XFW = [sba(f"XFW{i}", [128, D], F32, ch0 + 10 * D + i * 4 * D)
           for i in range(2)]
    MRW = sba("MRW", [1, NTOK], F32, ch0 + 10 * D)
    QSB = sba("QSB", [128, 512], BF16, ch0)
    QSQ = sba("QSQ", [128, 512], BF16, ch0 + 1024)
    KSQ = sba("KSQ", [128, 512], BF16, ch0 + 2048)
    VST = [sba(f"VST{i}", [128, 512], BF16, ch0 + 3072 + i * 1024)
           for i in range(2)]
    UST16 = [sba(f"UST16_{i}", [128, 512], F16, ch0 + i * 1024)
             for i in range(2)]
    DQmax = max(D, I3)
    DQF = [sba(f"DQF{i}", [128, DQmax], F32, ch0 + 20 * CW + i * 4 * DQmax)
           for i in range(2)]
    SCL4 = sba("SCL4", [128, 288], F32, 10240)
    QI5 = sba("QI5", [128, D], I32, 11392)
    OUTT = [sba(f"OUTT{i}", [128, D], F32, ch0 + 2 * D + i * 4 * D)
            for i in range(2)]
    assert 6144 + 2 * NTOK <= 16384
    a0 = ch_end
    XNT = sba("XNT", [128, DT * NTOK], BF16, a0)
    XQT = sba("XQT", [128, DT * NTOK], BF16, a0 + 2 * DT * NTOK)
    ATT = sba("ATT", [128, JT * 512], BF16, a0)
    WDT = sba("WDT", [128, IT * D], BF16, a0)
    WBFX = sba("WBFX", [128, DQmax], BF16, a0)
    WTS2 = [sba(f"WTS2_{i}", [128, 1024], BF16, a0 + 2 * DQmax + i * 2048)
            for i in range(2)]
    DQI = sba("DQI", [128, DQmax], I32, a0 + 2 * DQmax + 4096)
    qn2_off = a0 + 2 * DT * NTOK
    QN2 = sba("QN2", [1, NTOK], F32, qn2_off)
    assert 2 * DQmax + 4096 + 4 * DQmax <= 2 * DT * NTOK, "DQI overflow"
    cur = a0 + max(4 * DT * NTOK, JT * 1024, 2 * IT * D,
                   qn2_off - a0 + 4 * NTOK)
    b0 = cur
    KTL = sba("KTL", [128, DT * NTOK], BF16, b0)
    WT2 = [sba(f"WT2_{i}", [128, DT * D], BF16,
               b0 + 2 * DT * NTOK + i * 2 * DT * D) for i in range(2)]
    CSTG = [sba(f"CSTG{i}", [128, 2048], F32, b0 + i * 8192)
            for i in range(2)]
    KCB = [sba(f"KCB{i}", [128, DT * 512], BF16, b0 + i * DT * 1024)
           for i in range(2)]
    VCB = [sba(f"VCB{i}", [128, 4 * D], BF16, b0 + 2 * DT * 1024 + i * 8 * D)
           for i in range(2)]
    OT3 = sba("OT3", [128, DT * NTOK], BF16,
              b0 + 2 * DT * 1024 + 16 * D)
    OBUF = sba("OBUF", [128, IT * 512], BF16, b0)
    XRES = [sba(f"XRES{i}", [128, D], F32, b0 + IT * 1024 + i * 4 * D)
            for i in range(2)]
    assert IT * 1024 + 8 * D <= 2 * DT * 1024 + 16 * D, \
        "OBUF/XRES overlap OT3"
    cur = b0 + max(2 * DT * NTOK + 4 * DT * D,
                   2 * DT * 1024 + 16 * D + 2 * DT * NTOK)
    WOT = sba("WOT", [128, DT * D], BF16, cur)
    cur += 2 * DT * D
    assert cur <= SBUF_CAP, f"SBUF overflow: {cur}"

    PT = [nc.alloc_psum_tensor(f"PT{i}", [128, 128], BF16) for i in range(2)]
    PB = [nc.alloc_psum_tensor(f"PB{i}", [128, 512], F32) for i in range(6)]

    sems = ["tv", "vt", "sv", "vs", "ts", "st", "gv", "cc", "vv", "pv",
            "ev", "sm", "da", "db", "dc", "dd", "de", "df", "dg"]
    P = Prog(nc, sems)
    bounce_n = [0]

    def bcast(col, dcol):
        i = bounce_n[0]
        bounce_n[0] += 1
        assert i < 64
        P.wait("y", "vv")
        P.dma("y", sb_h[i:i + 1].unsqueeze(0), SC1[0:1, col:col + 1], ch="dg")
        P.wait("y", "dg")
        P.dma("y", SCAL[:, dcol:dcol + 1],
              sb_h[i:i + 1].unsqueeze(0).broadcast_to((128, 1)), ch="dg")

    def partmax_to_sc1(src_col, dst_col):
        P.wait("y", "vs")
        P.dma("y", pb_h.ap(), MAXC[:, src_col:src_col + 1], ch="dg")
        P.wait("y", "dg")
        P.dma("y", MXR.ap(), pb_h.ap().unsqueeze(0), ch="dg")
        P.wait("v", "dg")
        P.op("v", lambda v: v.tensor_reduce(
            SC1[0:1, dst_col:dst_col + 1], MXR.ap(), AX.X, ALU.max),
            inc="vv")

    def transpose_seq(src_getter, nblk, dst_writer):
        free_at = {}
        for i in range(nblk):
            pt = PT[i % 2]
            if i < 2:
                P.wait("t", "vt")
            if i >= 2:
                P.wait("t", "vt", free_at[i - 2])
            tk = P.op("t", lambda te, pt=pt, i=i: te.transpose(
                pt.ap(), src_getter(i), IDB.ap()), inc="tv")
            P.wait("v", "tv", tk)
            dst_writer(i, pt.ap())
            free_at[i] = P.counts["vt"]

    P.op("g", lambda g: g.memset(MG.ap(), MAGIC))
    P.op("g", lambda g: g.memset(EPSB.ap(), EPS_LN))
    P.op("g", lambda g: g.memset(ONESB.ap(), 1.0))
    P.op("g", lambda g: g.memset(ONE1B.ap(), 1.0))
    P.op("g", lambda g: g.memset(MAXC.ap(), 0.0))
    P.op("g", lambda g: g.memset(SC1.ap(), 0.0), inc="gv")
    P.dma("y", IDB.ap(), idb_d.ap(), ch="da")
    P.barrier()

    cast_done = {}

    def load_cast_w(src_d, dst, width, bufs, chw, chans):
        for i in range(width // chw):
            buf = bufs[i % 2]
            ch = chans[i % 2]
            if id(buf) in cast_done:
                P.wait("y", "sv", cast_done[id(buf)])
            ld = P.dma("y", buf.ap()[:, 0:chw],
                       src_d.ap()[:, chw * i:chw * (i + 1)], ch=ch)
            P.wait("s", ch, ld)
            cast_done[id(buf)] = P.op("s", lambda s, buf=buf, i=i, chw=chw,
                                      dst=dst: s.activation(
                dst[:, chw * i:chw * (i + 1)], buf.ap()[:, 0:chw],
                AFT.Identity), inc="sv")

    load_cast_w(wT_d["wq"], WT2[0], DT * D, CSTG, 2048, ("de", "df"))
    load_cast_w(wT_d["wk"], WT2[1], DT * D, CSTG, 2048, ("de", "df"))

    plan = [("q", t) for t in range(TT)] + [("f", t) for t in range(TT)]
    tile_done = {}
    fstore = {}
    for i, (kind, t) in enumerate(plan):
        buf = XTL[i % 2]
        src = xq_d if kind == "q" else xf_d
        if i >= 2:
            P.wait("y", "sv", tile_done[i - 2])
        lch = "da" if i % 2 == 0 else "db"
        ld = P.dma("y", buf.ap(), src.ap()[128 * t:128 * (t + 1), :], ch=lch)
        c0 = 8 * (i % 2)
        P.wait("v", lch, ld)
        P.op("v", lambda v, b=buf, c=c0: v.tensor_reduce(
            STAT[:, c:c + 1], b.ap(), AX.X, ALU.add), inc="vs")
        P.wait("s", lch, ld)
        P.wait("s", "vs")
        s1 = P.op("s", lambda s, b=buf: s.activation(
            X2S[:, 0:D], b.ap(), AFT.Square), inc="sv")
        P.wait("v", "sv", s1)
        P.op("v", lambda v, c=c0: v.tensor_reduce(
            STAT[:, c + 1:c + 2], X2S[:, 0:D], AX.X, ALU.add), inc="vs")
        P.wait("v", "vs")
        P.op("v", lambda v, c=c0: v.tensor_scalar(
            STAT[:, c + 2:c + 3], STAT[:, c:c + 1], 1.0 / D, None, ALU.mult),
            inc="vv")
        P.wait("v", "vv")
        P.op("v", lambda v, c=c0: v.tensor_tensor(
            STAT[:, c + 3:c + 4], STAT[:, c + 2:c + 3], STAT[:, c + 2:c + 3],
            ALU.mult), inc="vv")
        P.wait("v", "vv")
        v4 = P.op("v", lambda v, c=c0: v.scalar_tensor_tensor(
            STAT[:, c + 4:c + 5], STAT[:, c + 1:c + 2], 1.0 / D,
            STAT[:, c + 3:c + 4], ALU.mult, ALU.subtract), inc="vs")
        P.wait("s", "vs", v4)
        s2 = P.op("s", lambda s, c=c0: s.activation(
            STAT[:, c + 5:c + 6], STAT[:, c + 4:c + 5], AFT.Sqrt,
            bias=EPSB.ap()), inc="sv")
        P.wait("v", "sv", s2)
        P.op("v", lambda v, c=c0: v.reciprocal(
            STAT[:, c + 6:c + 7], STAT[:, c + 5:c + 6]), inc="vv")
        P.wait("v", "vv")
        v5 = P.op("v", lambda v, c=c0: v.scalar_tensor_tensor(
            STAT[:, c + 7:c + 8], STAT[:, c + 2:c + 3], -1.0,
            STAT[:, c + 6:c + 7], ALU.mult, ALU.mult), inc="vs")
        P.wait("s", "vs", v5)
        dstb = XNQB if kind == "q" else XFW[t % 2]
        if kind == "f" and t >= 2:
            P.wait("s", "dc" if (t - 2) % 2 == 0 else "dd", fstore[t - 2])
        if kind == "q" and i >= 1:
            P.wait("s", "vt")
        s3 = P.op("s", lambda s, b=buf, db=dstb, c=c0: s.activation(
            db.ap(), b.ap(), AFT.Identity, bias=STAT[:, c + 7:c + 8],
            scale=STAT[:, c + 6:c + 7]), inc="sv")
        tile_done[i] = s3
        if kind == "f":
            P.wait("v", "sv", s3)
            P.op("v", lambda v, db=dstb, t=t: v.tensor_reduce(
                MAXC[:, t:t + 1], db.ap(), AX.X, ALU.max,
                apply_absolute_value=True), inc="vs")
            P.wait("y", "sv", s3)
            fstore[t] = P.dma("y", xnf_h.ap()[t], dstb.ap(),
                              ch="dc" if t % 2 == 0 else "dd")
        else:
            P.wait("t", "sv", s3)

            def wr(dtile, pap, t=t):
                P.op("v", lambda v, pap=pap, dtile=dtile, t=t: v.tensor_copy(
                    XNT[:, dtile * NTOK + 128 * t:
                        dtile * NTOK + 128 * (t + 1)], pap), inc="vt")
            transpose_seq(lambda dtile: XNQB[:, 128 * dtile:
                                             128 * (dtile + 1)], DT, wr)
    P.barrier()

    P.barrier()

    bank_free = {}

    def use_bank(bi, eng="t"):
        if bi in bank_free:
            sem, cnt = bank_free[bi]
            P.wait(eng, sem, cnt)

    def proj_fm(wT, ch, et, bank, inc):
        for dtile in range(DT):
            P.op("t", lambda te, b=bank, d2=dtile, et=et, ch=ch, wT=wT:
                 te.matmul(b.ap(),
                           wT[:, d2 * D + 128 * et:d2 * D + 128 * (et + 1)],
                           XNT[:, d2 * NTOK + 512 * ch:
                               d2 * NTOK + 512 * (ch + 1)],
                           start=(d2 == 0), stop=(d2 == DT - 1)),
                 inc=inc if dtile == DT - 1 else None)
        return P.counts[inc]

    for ch in range(TCH):
        for et in range(DT):
            bank = PB[et % 2]
            use_bank(et % 2)
            mmq = proj_fm(WT2[0], ch, et, bank, "tv")
            P.wait("v", "tv", mmq)
            P.wait("v", "dc")
            P.wait("v", "ts")
            evq = P.op("v", lambda v, b=bank: v.tensor_scalar(
                QSB.ap(), b.ap(), 0.125, None, ALU.mult), inc="vt")
            bank_free[et % 2] = ("vt", evq)
            P.wait("s", "vt", evq)
            sq = P.op("s", lambda s: s.activation(
                QSQ.ap(), QSB.ap(), AFT.Square), inc="sv")
            P.wait("y", "vt", evq)
            P.dma("y", q_h.ap()[:, et, 512 * ch:512 * (ch + 1)], QSB.ap(),
                  ch="dc")
            P.wait("t", "sv", sq)
            use_bank(2)
            nm = P.op("t", lambda te, et=et: te.matmul(
                PB[2][0:1, :], ONESB.ap(), QSQ.ap(),
                start=(et == 0), stop=(et == DT - 1)), inc="ts")
        P.wait("v", "ts")
        qe = P.op("v", lambda v, ch=ch: v.tensor_copy(
            QN2[0:1, 512 * ch:512 * (ch + 1)], PB[2][0:1, :]), inc="vt")
        bank_free[2] = ("vt", qe)
    P.barrier()
    for ch in range(TCH):
        for et in range(DT):
            bank = PB[et % 2]
            use_bank(et % 2)
            mmk = proj_fm(WT2[1], ch, et, bank, "tv")
            P.wait("v", "tv", mmk)
            evk = P.op("v", lambda v, b=bank, et=et, ch=ch: v.tensor_copy(
                KTL[:, et * NTOK + 512 * ch:et * NTOK + 512 * (ch + 1)],
                b.ap()), inc="vt")
            bank_free[et % 2] = ("vt", evk)
    P.wait("s", "vt")
    for ch in range(TCH):
        for et in range(DT):
            P.wait("s", "ts")
            sq = P.op("s", lambda s, et=et, ch=ch: s.activation(
                KSQ.ap(), KTL[:, et * NTOK + 512 * ch:
                              et * NTOK + 512 * (ch + 1)], AFT.Square),
                inc="sv")
            P.wait("t", "sv", sq)
            use_bank(2)
            P.op("t", lambda te, et=et: te.matmul(
                PB[2][0:1, :], ONESB.ap(), KSQ.ap(),
                start=(et == 0), stop=(et == DT - 1)), inc="ts")
        P.wait("v", "ts")
        ke = P.op("v", lambda v, ch=ch: v.tensor_copy(
            MRW[0:1, 512 * ch:512 * (ch + 1)], PB[2][0:1, :]), inc="vt")
        bank_free[2] = ("vt", ke)
    P.wait("v", "vt")
    P.op("v", lambda v: v.tensor_reduce(
        SC1[0:1, 1:2], MRW.ap(), AX.X, ALU.max), inc="vv")
    P.wait("y", "vt")
    P.dma("y", kvin_h[0:KVK].rearrange("(p a b) -> p a b", p=128, a=DT),
          KTL.ap().rearrange("p (a b) -> p a b", b=NTOK), ch="de")
    P.barrier()
    load_cast_w(wT_d["wv"], WT2[0], DT * D, XTL, 1024, ("da", "db"))
    P.barrier()
    for jt in range(TT):
        for cch in range(EH):
            k = jt * EH + cch
            bank = PB[k % 2]
            use_bank(k % 2)
            for dtile in range(DT):
                P.op("t", lambda te, b=bank, d2=dtile, jt=jt, cch=cch:
                     te.matmul(b.ap()[:, 0:EW],
                               XNT[:, d2 * NTOK + 128 * jt:
                                   d2 * NTOK + 128 * (jt + 1)],
                               WT2[0][:, d2 * D + 512 * cch:
                                      d2 * D + 512 * cch + EW],
                               start=(d2 == 0), stop=(d2 == DT - 1)),
                     inc="tv" if dtile == DT - 1 else None)
            P.wait("v", "tv")
            P.wait("v", "dc" if k % 2 == 0 else "dd")
            ev = P.op("v", lambda v, b=bank, k=k: v.tensor_copy(
                VST[k % 2].ap()[:, 0:EW], b.ap()[:, 0:EW]), inc="vt")
            bank_free[k % 2] = ("vt", ev)
            P.wait("y", "vt", ev)
            P.dma("y", kvin_h[KVK:KVK + KVV].rearrange(
                "(p a b) -> p a b", p=128, a=TT)[:, jt,
                                                 512 * cch:512 * cch + EW],
                  VST[k % 2].ap()[:, 0:EW],
                  ch="dc" if k % 2 == 0 else "dd")
    P.wait("v", "vs")
    P.op("v", lambda v: v.tensor_reduce(
        MAXC[:, 30:31], MAXC[:, 0:TT], AX.X, ALU.max), inc="vs")
    partmax_to_sc1(30, 0)
    P.wait("y", "vv")
    P.dma("y", cc_in[0].ap().unsqueeze(0), SC1[0:1, 0:8], ch="df")
    P.barrier()

    ncores = 2 * n_group
    groups8 = [list(range(ncores))]
    kvgroups = [list(range(n_group)), list(range(n_group, ncores))]
    P.wait("g", "df")
    P.wait("g", "de")
    P.wait("g", "dc")
    P.wait("g", "dd")
    cc1n = P.op("g", lambda g: g.collective_compute(
        "AllReduce", ALU.max, replica_groups=groups8,
        ins=[cc_in[0].ap().opt()], outs=[cc_out[0].ap().opt()]), inc="cc")
    P.op("g", lambda g: g.collective_compute(
        "AllGather", ALU.bypass, replica_groups=kvgroups,
        ins=[kvin_h.ap().opt()], outs=[kvout_h.ap().opt()]), inc="cc")

    dqf_done = {}

    def prep_ffn_w(src_d, rows, rowlen, dstT_h):
        X2S = SCL4
        nb = rowlen // 32
        for rt in range(rows):
            buf = DQF[rt % 2]
            wch = "da" if rt % 2 == 0 else "db"
            if id(buf) in dqf_done:
                P.wait("y", "vs", dqf_done[id(buf)])
            ld = P.dma("y", buf[:, 0:rowlen],
                       src_d.ap()[128 * rt:128 * (rt + 1), :], ch=wch)
            P.wait("v", wch, ld)
            r1 = P.op("v", lambda v, nb=nb, rl=rowlen, buf=buf: v.tensor_reduce(
                X2S[:, 0:nb],
                buf[:, 0:rl].rearrange("p (a b) -> p a b", b=32),
                AX.X, ALU.max, apply_absolute_value=True), inc="vv")
            P.wait("v", "vv", r1)
            P.op("v", lambda v, nb=nb: v.reciprocal(
                X2S[:, nb:2 * nb], X2S[:, 0:nb]), inc="vv")
            P.wait("v", "vv")
            P.op("v", lambda v, nb=nb: v.tensor_scalar(
                X2S[:, nb:2 * nb], X2S[:, nb:2 * nb], 7.0, 1e12,
                ALU.mult, ALU.min), inc="vv")
            P.op("v", lambda v, nb=nb: v.tensor_scalar(
                X2S[:, 2 * nb:3 * nb], X2S[:, 0:nb], 1.0 / 7.0, 1e-12,
                ALU.mult, ALU.max), inc="vv")
            P.wait("v", "vv")
            P.op("v", lambda v, nb=nb, rl=rowlen, buf=buf: v.tensor_tensor(
                buf[:, 0:rl].rearrange("p (a b) -> p a b", b=32),
                buf[:, 0:rl].rearrange("p (a b) -> p a b", b=32),
                X2S[:, nb:2 * nb].unsqueeze(-1).broadcast_to((128, nb, 32)),
                ALU.mult), inc="vv")
            P.wait("v", "vv")
            P.op("v", lambda v, rl=rowlen, buf=buf: v.tensor_scalar(
                DQI[:, 0:rl], buf[:, 0:rl], MAGIC, MAGIC,
                ALU.add, ALU.subtract), inc="vv")
            P.wait("v", "vv")
            P.wait("v", "tv")
            cv = P.op("v", lambda v, nb=nb, rl=rowlen: v.tensor_tensor(
                WBFX[:, 0:rl].rearrange("p (a b) -> p a b", b=32),
                DQI[:, 0:rl].rearrange("p (a b) -> p a b", b=32),
                X2S[:, 2 * nb:3 * nb].unsqueeze(-1).broadcast_to(
                    (128, nb, 32)), ALU.mult), inc="vs")
            dqf_done[id(buf)] = cv
            P.wait("t", "vs", cv)
            nblk = rowlen // 128

            def wr(bi, pap, rt=rt, dstT_h=dstT_h, nblk=nblk):
                wts = WTS2[(bi // 8) % 2]
                if bi % 8 == 0:
                    P.wait("v", "dc" if (bi // 8) % 2 == 0 else "dd")
                P.op("v", lambda v, pap=pap, bi=bi, wts=wts: v.tensor_copy(
                    wts[:, 128 * (bi % 8):128 * (bi % 8 + 1)], pap),
                    inc="vt")
                if bi % 8 == 7 or bi == nblk - 1:
                    base = (bi // 8) * 8
                    cnt = bi - base + 1
                    P.wait("y", "vt")
                    P.dma("y", dstT_h.ap()[:, base:base + cnt,
                                           128 * rt:128 * (rt + 1)],
                          wts[:, 0:cnt * 128].rearrange(
                              "p (a b) -> p a b", b=128),
                          ch="dc" if (bi // 8) % 2 == 0 else "dd")
            transpose_seq(lambda bi: WBFX[:, 128 * bi:128 * (bi + 1)],
                          nblk, wr)
            yield

    done_p = {}

    def gen_preps():
        yield from prep_ffn_w(wu_d, IT, D, wuT_h)
        done_p["wu"] = (P.counts["dc"], P.counts["dd"])
        yield from prep_ffn_w(wg_d, IT, D, wgT_h)
        done_p["wg"] = (P.counts["dc"], P.counts["dd"])
        yield from prep_ffn_w(wd_d, DT, I3, wdT_h)
        load_cast_w(wT_d["wo"], WOT, DT * D, XTL, 1024, ("df", "dg"))

    P.wait("y", "cc", cc1n)
    P.dma("y", SC1[0:1, 8:16], cc_out[0].ap().unsqueeze(0), ch="df")
    P.wait("v", "df")
    P.op("v", lambda v: v.tensor_scalar(
        SC1[0:1, 16:17], SC1[0:1, 8:9], INV_QA, 1e-12, ALU.mult, ALU.max),
        inc="vv")
    P.op("v", lambda v: v.reciprocal(SC1[0:1, 17:18], SC1[0:1, 8:9]),
         inc="vv")
    P.wait("v", "vv")
    P.op("v", lambda v: v.tensor_scalar(
        SC1[0:1, 17:18], SC1[0:1, 17:18], 32767.0, 1e12, ALU.mult, ALU.min),
        inc="vv")
    P.wait("v", "vv")
    P.op("v", lambda v: v.tensor_scalar(
        MRW.ap(), QN2.ap(), SC1[0:1, 9:10], None, ALU.mult), inc="vs")
    P.wait("s", "vs")
    P.op("s", lambda s: s.activation(MRW.ap(), MRW.ap(), AFT.Sqrt),
         inc="sv")
    P.wait("v", "sv")
    P.op("v", lambda v: v.tensor_scalar(
        QAR.ap(), MRW.ap(), -1.0, -MSLACK, ALU.mult, ALU.add), inc="vv")
    bcast(17, 0)
    bcast(16, 7)

    def gen_b56():
        P.wait("y", "vv")
        P.wait("s", "dg")
        P.wait("v", "dg")
        for t in range(TT):
            buf = XFW[t % 2]
            if t >= 2:
                P.wait("y", "sv")
            lch = "da" if t % 2 == 0 else "db"
            ld = P.dma("y", buf.ap(), xnf_h.ap()[t], ch=lch)
            P.wait("s", lch, ld)
            P.wait("s", "vv")
            sa = P.op("s", lambda s, b=buf: s.activation(
                X2S.ap()[:, 0:D], b.ap(), AFT.Identity, bias=MG.ap(),
                scale=SCAL[:, 0:1]), inc="sv")
            P.wait("v", "sv", sa)
            P.wait("v", "vs")
            P.op("v", lambda v: v.tensor_scalar(
                QI5.ap(), X2S.ap()[:, 0:D], MAGIC, None, ALU.subtract),
                inc="vv")
            P.wait("v", "vv")
            P.wait("v", "tv")
            cb = P.op("v", lambda v: v.tensor_copy(XNQB.ap(), QI5.ap()),
                      inc="vs")
            P.wait("t", "vs", cb)

            def wr(dtile, pap, t=t):
                P.op("v", lambda v, pap=pap, dtile=dtile, t=t: v.tensor_copy(
                    XQT[:, dtile * NTOK + 128 * t:dtile * NTOK + 128 * (t + 1)],
                    pap), inc="vt")
            transpose_seq(lambda dtile: XNQB[:, 128 * dtile:128 * (dtile + 1)],
                          DT, wr)
            yield

        P.op("g", lambda g: g.memset(MAXC[:, 28:29], 0.0))
        P.op("g", lambda g: g.memset(MAXC[:, 29:30], 0.0), inc="gv")
        P.wait("v", "gv")
        P.wait("t", "vt")
        bank_free.clear()
        for wtag, srcT_h, dst_h, mxcol in (("u", wuT_h, up_h, 28),
                                           ("g", wgT_h, ga_h, 29)):
            key = "wu" if wtag == "u" else "wg"
            while key not in done_p:
                yield
            P.wait("y", "dc", done_p[key][0])
            P.wait("y", "dd", done_p[key][1])
            for it in range(IT):
                wub = WT2[it % 2]
                P.wait("y", "tv")
                wch = "da" if it % 2 == 0 else "db"
                ld = P.dma("y", wub[:, 0:DT * 128].rearrange("p (a b) -> p a b",
                                                             b=128),
                           srcT_h.ap()[:, :, 128 * it:128 * (it + 1)], ch=wch)
                P.wait("t", wch, ld)
                for ch in range(TCH):
                    k = it * TCH + ch
                    bank = PB[k % 2]
                    use_bank(k % 2)
                    for ht in range(DT):
                        P.op("t", lambda te, b=bank, ht=ht, ch=ch, wub=wub:
                             te.matmul(b.ap(),
                                       wub[:, 128 * ht:128 * (ht + 1)],
                                       XQT[:, ht * NTOK + 512 * ch:
                                           ht * NTOK + 512 * (ch + 1)],
                                       start=(ht == 0), stop=(ht == DT - 1)),
                             inc="tv" if ht == DT - 1 else None)
                    u16 = UST16[k % 2]
                    P.wait("v", "tv")
                    P.wait("v", "dc" if k % 2 == 0 else "dd")
                    ev = P.op("v", lambda v, b=bank, u16=u16: v.tensor_scalar(
                        u16.ap(), b.ap(), SCAL[:, 7:8], None, ALU.mult),
                        inc="vt")
                    P.wait("v", "vt", ev)
                    mx = P.op("v", lambda v, b=bank: v.tensor_reduce(
                        STAT[:, 8:9], b.ap(), AX.X, ALU.max,
                        apply_absolute_value=True), inc="vv")
                    bank_free[k % 2] = ("vv", mx)
                    P.wait("v", "vv", mx)
                    P.op("v", lambda v, mxcol=mxcol: v.tensor_tensor(
                        MAXC[:, mxcol:mxcol + 1], MAXC[:, mxcol:mxcol + 1],
                        STAT[:, 8:9], ALU.max), inc="vs")
                    P.wait("y", "vt", ev)
                    P.dma("y", dst_h.ap()[it][:, 512 * ch:512 * (ch + 1)],
                          u16.ap(), ch="dc" if k % 2 == 0 else "dd")
                yield
            partmax_to_sc1(mxcol, 2 if wtag == "u" else 3)

    _SP = object()
    gp_, g56_ = gen_preps(), gen_b56()
    dp_ = d56_ = False
    while not (dp_ and d56_):
        if not dp_:
            dp_ = next(gp_, _SP) is _SP
        if not d56_:
            d56_ = next(g56_, _SP) is _SP
    P.wait("y", "vv")
    P.dma("y", cc_in[1].ap().unsqueeze(0), SC1[0:1, 0:8], ch="df")
    P.barrier()
    cc2n = P.op("g", lambda g: g.collective_compute(
        "AllReduce", ALU.max, replica_groups=groups8,
        ins=[cc_in[1].ap().opt()], outs=[cc_out[1].ap().opt()]), inc="cc")
    P.wait("y", "cc", cc2n)
    P.dma("y", SC1[0:1, 8:16], cc_out[1].ap().unsqueeze(0), ch="df")

    def gen_chains():
        P.wait("v", "df")
        P.op("v", lambda v: v.tensor_tensor(
            SC1[0:1, 18:19], SC1[0:1, 16:17], SC1[0:1, 10:11], ALU.mult),
            inc="vv")
        P.op("v", lambda v: v.tensor_tensor(
            SC1[0:1, 19:20], SC1[0:1, 16:17], SC1[0:1, 11:12], ALU.mult),
            inc="vv")
        P.wait("v", "vv")
        P.op("v", lambda v: v.tensor_scalar(
            SC1[0:1, 20:21], SC1[0:1, 18:19], INV_QA, 1e-12, ALU.mult,
            ALU.max), inc="vv")
        P.op("v", lambda v: v.tensor_scalar(
            SC1[0:1, 22:23], SC1[0:1, 19:20], INV_QA, 1e-12, ALU.mult,
            ALU.max), inc="vv")
        P.wait("v", "vv")
        P.op("v", lambda v: v.reciprocal(SC1[0:1, 21:22], SC1[0:1, 20:21]),
             inc="vv")
        P.wait("v", "vv")
        P.op("v", lambda v: v.reciprocal(SC1[0:1, 23:24], SC1[0:1, 22:23]),
             inc="vv")
        P.wait("v", "vv")
        bcast(23, 1)
        bcast(22, 2)
        bcast(21, 3)
        P.op("g", lambda g: g.memset(MAXC[:, 27:28], 0.0), inc="gv")
        P.wait("v", "gv")
        yield
        for it in range(IT):
            for h in range(2):
                sl = slice(CW * h, CW * (h + 1))
                P.wait("y", "vs")
                ld = P.dma("y", CIN.ap(), ga_h.ap()[it][:, sl], ch="df")
                P.wait("v", "df", ld)
                P.op("v", lambda v: v.tensor_scalar(
                    CF1.ap(), CIN.ap(), SCAL[:, 1:2], MAGIC,
                    ALU.mult, ALU.add), inc="vv")
                P.wait("v", "vv")
                c1 = P.op("v", lambda v: v.tensor_scalar(
                    CI1.ap(), CF1.ap(), MAGIC, None, ALU.subtract),
                    inc="vs")
                yield
                P.wait("s", "vs", c1)
                f2 = P.op("s", lambda s: s.activation(
                    CF2.ap(), CI1.ap(), AFT.Sigmoid, scale=SCAL[:, 2:3]),
                    inc="sv")
                P.wait("v", "sv", f2)
                P.op("v", lambda v: v.tensor_scalar(
                    CF1.ap(), CF2.ap(), 65536.0, MAGIC,
                    ALU.mult, ALU.add), inc="vv")
                P.wait("v", "vv")
                P.op("v", lambda v: v.tensor_scalar(
                    CI2.ap(), CF1.ap(), MAGIC, None, ALU.subtract),
                    inc="vv")
                yield
                P.wait("v", "vv")
                P.op("v", lambda v: v.scalar_tensor_tensor(
                    CF2.ap(), CI2.ap(), 65535.0, CI1.ap(), ALU.min,
                    ALU.mult), inc="vv")
                P.wait("v", "vv")
                P.wait("v", "de")
                ev = P.op("v", lambda v: v.tensor_scalar(
                    CO16.ap(), CF2.ap(), 2.0 ** -16, None, ALU.mult),
                    inc="vv")
                P.wait("v", "vv", ev)
                P.op("v", lambda v: v.tensor_reduce(
                    STAT[:, 8:9], CO16.ap(), AX.X, ALU.max,
                    apply_absolute_value=True), inc="vv")
                P.wait("v", "vv")
                vb = P.op("v", lambda v: v.tensor_tensor(
                    MAXC[:, 27:28], MAXC[:, 27:28], STAT[:, 8:9], ALU.max),
                    inc="vs")
                P.wait("y", "vs", vb)
                P.dma("y", p_h.ap()[it][:, sl], CO16.ap(), ch="de")
                yield
        partmax_to_sc1(27, 4)
        P.wait("y", "vv")
        P.dma("y", cc_in[2].ap().unsqueeze(0), SC1[0:1, 0:8], ch="df")
        for _ch in Prog.DMA_CHANNELS:
            P.wait("g", _ch)
        cc3n = P.op("g", lambda g: g.collective_compute(
            "AllReduce", ALU.max, replica_groups=groups8,
            ins=[cc_in[2].ap().opt()], outs=[cc_out[2].ap().opt()]),
            inc="cc")
        P.wait("y", "cc", cc3n)
        P.dma("y", SC1[0:1, 8:16], cc_out[2].ap().unsqueeze(0), ch="df")
        yield
        P.wait("v", "df")
        P.op("v", lambda v: v.tensor_tensor(
            SC1[0:1, 18:19], SC1[0:1, 22:23], SC1[0:1, 12:13], ALU.mult),
            inc="vv")
        P.wait("v", "vv")
        P.op("v", lambda v: v.tensor_scalar(
            SC1[0:1, 26:27], SC1[0:1, 18:19], INV_QA, 1e-12, ALU.mult,
            ALU.max), inc="vv")
        P.wait("v", "vv")
        P.op("v", lambda v: v.reciprocal(SC1[0:1, 27:28], SC1[0:1, 26:27]),
             inc="vv")
        P.wait("v", "vv")
        P.op("v", lambda v: v.tensor_tensor(
            SC1[0:1, 28:29], SC1[0:1, 22:23], SC1[0:1, 27:28], ALU.mult),
            inc="vv")
        bcast(28, 4)
        P.op("g", lambda g: g.memset(MAXC[:, 26:27], 0.0), inc="gv")
        P.wait("v", "gv")
        P.wait("y", "de")
        yield
        for it in range(IT):
            for h in range(2):
                sl = slice(CW * h, CW * (h + 1))
                P.wait("y", "vs")
                l1 = P.dma("y", CIN.ap(), up_h.ap()[it][:, sl], ch="df")
                l2 = P.dma("y", CIN2.ap(), p_h.ap()[it][:, sl], ch="dg")
                P.wait("v", "df", l1)
                P.op("v", lambda v: v.tensor_scalar(
                    CF1.ap(), CIN.ap(), SCAL[:, 3:4], MAGIC,
                    ALU.mult, ALU.add), inc="vv")
                P.wait("v", "vv")
                P.op("v", lambda v: v.tensor_scalar(
                    CI1.ap(), CF1.ap(), MAGIC, None, ALU.subtract),
                    inc="vs")
                yield
                P.wait("v", "dg", l2)
                P.op("v", lambda v: v.tensor_scalar(
                    CF2.ap(), CIN2.ap(), SCAL[:, 4:5], MAGIC,
                    ALU.mult, ALU.add), inc="vv")
                P.wait("v", "vv")
                P.op("v", lambda v: v.tensor_scalar(
                    CI2.ap(), CF2.ap(), MAGIC, None, ALU.subtract),
                    inc="vv")
                yield
                P.wait("v", "vv")
                P.op("v", lambda v: v.tensor_tensor(
                    CF1.ap(), CI1.ap(), CI2.ap(), ALU.mult), inc="vv")
                P.wait("v", "vv")
                P.wait("v", "de")
                ev = P.op("v", lambda v: v.tensor_scalar(
                    CO16.ap(), CF1.ap(), 2.0 ** -15, None, ALU.mult),
                    inc="vv")
                P.wait("v", "vv", ev)
                P.op("v", lambda v: v.tensor_reduce(
                    STAT[:, 8:9], CO16.ap(), AX.X, ALU.max,
                    apply_absolute_value=True), inc="vv")
                P.wait("v", "vv")
                vb = P.op("v", lambda v: v.tensor_tensor(
                    MAXC[:, 26:27], MAXC[:, 26:27], STAT[:, 8:9], ALU.max),
                    inc="vs")
                P.wait("y", "vs", vb)
                P.dma("y", q2_h.ap()[it][:, sl], CO16.ap(), ch="de")
                yield
        partmax_to_sc1(26, 5)
        P.wait("y", "vv")
        P.dma("y", cc_in[3].ap().unsqueeze(0), SC1[0:1, 0:8], ch="df")
        for _ch in Prog.DMA_CHANNELS:
            P.wait("g", _ch)
        cc4n = P.op("g", lambda g: g.collective_compute(
            "AllReduce", ALU.max, replica_groups=groups8,
            ins=[cc_in[3].ap().opt()], outs=[cc_out[3].ap().opt()]),
            inc="cc")
        P.wait("y", "cc", cc4n)
        P.dma("y", SC1[0:1, 8:16], cc_out[3].ap().unsqueeze(0), ch="df")
        yield
        P.wait("v", "df")
        P.op("v", lambda v: v.tensor_tensor(
            SC1[0:1, 18:19], SC1[0:1, 20:21], SC1[0:1, 26:27], ALU.mult),
            inc="vv")
        P.wait("v", "vv")
        P.op("v", lambda v: v.tensor_scalar(
            SC1[0:1, 18:19], SC1[0:1, 18:19], 2.0 ** 15, None, ALU.mult),
            inc="vv")
        P.wait("v", "vv")
        P.op("v", lambda v: v.tensor_tensor(
            SC1[0:1, 19:20], SC1[0:1, 18:19], SC1[0:1, 13:14], ALU.mult),
            inc="vv")
        P.wait("v", "vv")
        P.op("v", lambda v: v.tensor_scalar(
            SC1[0:1, 29:30], SC1[0:1, 19:20], INV_QA, 1e-12, ALU.mult,
            ALU.max), inc="vv")
        P.wait("v", "vv")
        P.op("v", lambda v: v.reciprocal(SC1[0:1, 30:31], SC1[0:1, 29:30]),
             inc="vv")
        P.wait("v", "vv")
        P.op("v", lambda v: v.tensor_tensor(
            SC1[0:1, 31:32], SC1[0:1, 18:19], SC1[0:1, 30:31], ALU.mult),
            inc="vv")
        bcast(31, 5)
        bcast(29, 6)
        P.wait("y", "de")
        yield
        for it in range(IT):
            for h in range(2):
                sl = slice(CW * h, CW * (h + 1))
                P.wait("y", "vv")
                ld = P.dma("y", CIN.ap(), q2_h.ap()[it][:, sl], ch="df")
                P.wait("v", "df", ld)
                P.op("v", lambda v: v.tensor_scalar(
                    CF1.ap(), CIN.ap(), SCAL[:, 5:6], MAGIC,
                    ALU.mult, ALU.add), inc="vv")
                P.wait("v", "vv")
                P.op("v", lambda v: v.tensor_scalar(
                    CI1.ap(), CF1.ap(), MAGIC, None, ALU.subtract),
                    inc="vv")
                yield
                P.wait("v", "vv")
                P.wait("v", "de")
                cb = P.op("v", lambda v: v.tensor_copy(OBB.ap(), CI1.ap()),
                          inc="vv")
                P.wait("y", "vv", cb)
                P.dma("y", o_h.ap()[:, it, sl], OBB.ap(), ch="de")
                yield

    def gen_b10():
        pv_prev = 0
        ts_after = {}
        pv_after = {}
        for qc in range(NCH):
            P.wait("y", "ts")
            lq = P.dma("y", QCB.ap().rearrange("p (a b) -> p a b", b=512),
                       q_h.ap()[:, :, 512 * qc:512 * (qc + 1)], ch="de")
            P.wait("t", "de", lq)
            exp_base = P.counts["st"]
            if qc > 0:
                P.wait("t", "st", exp_base)
                P.wait("t", "sm")
            for jc in range(JCH):
                r = jc // (NTOK // 512)
                c = jc % (NTOK // 512)
                kb = KCB[jc % 2]
                if jc >= 2:
                    P.wait("y", "ts", ts_after[(qc, jc - 2)])
                kch = "da" if jc % 2 == 0 else "db"
                lk = P.dma("y", kb.ap().rearrange("p (a b) -> p a b", b=512),
                           kvout_h.ap()[r][0:KVK].rearrange(
                               "(p a b) -> p a b", p=128, a=DT)[
                               :, :, 512 * c:512 * (c + 1)], ch=kch)
                P.wait("t", kch, lk)
                for j4 in range(4):
                    j = jc * 4 + j4
                    bank = PB[4 + (j % 2)]
                    if j >= 2:
                        P.wait("t", "st", exp_base + j - 1)
                    P.op("t", lambda te, b=bank, qc=qc: te.matmul(
                        b.ap(), ONE1B.ap(),
                        QAR[0:1, 512 * qc:512 * (qc + 1)],
                        start=True, stop=False))
                    for et in range(DT):
                        P.op("t", lambda te, b=bank, kb=kb, et=et, j4=j4:
                             te.matmul(b.ap(),
                                       kb[:, et * 512 + 128 * j4:
                                          et * 512 + 128 * (j4 + 1)],
                                       QCB[:, et * 512:(et + 1) * 512],
                                       start=False, stop=(et == DT - 1)),
                             inc="ts" if et == DT - 1 else None)
                    P.wait("s", "ts")
                    if qc > 0 and j == 0:
                        P.wait("s", "pv", pv_prev)
                    P.op("s", lambda s, b=bank, j=j: s.activation(
                        ATT[:, j * 512:(j + 1) * 512], b.ap(), AFT.Exp),
                        inc="st")
                ts_after[(qc, jc)] = P.counts["ts"]
                yield
            P.wait("t", "st")
            if qc > 0:
                P.wait("t", "ev")
            else:
                P.wait("t", "vv")
                P.wait("t", "vt")
            for half in range(EH):
                if half > 0:
                    P.wait("t", "ev")
                for jc in range(JCH):
                    r = jc // (NTOK // 512)
                    c = jc % (NTOK // 512)
                    vb = VCB[jc % 2]
                    if jc >= 2:
                        P.wait("y", "pv", pv_after[(qc, half, jc - 2)])
                    if jc < 2:
                        P.wait("y", "pv")
                    vch = "dc" if jc % 2 == 0 else "dd"
                    lv = P.dma("y", vb.ap()[:, 0:4 * EW].rearrange(
                        "p (a b) -> p a b", b=EW),
                        kvout_h.ap()[r][KVK:KVK + KVV].rearrange(
                            "(p a b) -> p a b", p=128, a=TT)[
                            :, 4 * c:4 * (c + 1),
                            512 * half:512 * half + EW], ch=vch)
                    P.wait("t", vch, lv)
                    for j4 in range(4):
                        j = jc * 4 + j4
                        for qt in range(4):
                            last = (j4 == 3 and qt == 3)
                            P.op("t", lambda te, j=j, qt=qt, vb=vb, j4=j4:
                                 te.matmul(PB[qt].ap()[:, 0:EW],
                                           ATT[:, j * 512 + 128 * qt:
                                               j * 512 + 128 * (qt + 1)],
                                           vb[:, j4 * EW:(j4 + 1) * EW],
                                           start=(j == 0),
                                           stop=(j == JT - 1)),
                                 inc="pv" if (last and half > 0) else None)
                            if half == 0:
                                P.op("t", lambda te, j=j, qt=qt: te.matmul(
                                    PB[4][:, qt:qt + 1],
                                    ATT[:, j * 512 + 128 * qt:
                                        j * 512 + 128 * (qt + 1)],
                                    ONESB.ap(),
                                    start=(j == 0 and qt == 0),
                                    stop=(j == JT - 1 and qt == 3)),
                                    inc="pv" if last else None)
                    pv_after[(qc, half, jc)] = P.counts["pv"]
                    yield
                P.wait("v", "pv")
                P.wait("s", "pv")
                if half == 0:
                    P.op("v", lambda v: v.tensor_copy(SUMS.ap(),
                                                      PB[4][:, 0:4]),
                         inc="sm")
                    P.wait("v", "sm")
                    P.op("v", lambda v: v.reciprocal(RECS.ap(), SUMS.ap()),
                         inc="sm")
                    P.wait("v", "sm")
                P.wait("s", "sm")
                for qt in range(4):
                    rloc = qc * 512 + qt * 128
                    vblk = rloc // D
                    d0t = (rloc % D) // 128
                    col0 = d0t * NTOK + NV * 512 * half + vblk
                    P.op("s", lambda s, qt=qt, col0=col0:
                         s.activation(
                             OT3[:, col0:col0 + NV * (EW - 1) + 1:NV],
                             PB[qt].ap()[:, 0:EW], AFT.Identity,
                             scale=RECS[:, qt:qt + 1]), inc="ev")
                yield
            pv_prev = P.counts["pv"]

    _STOP = object()
    gc_, gb_ = gen_chains(), gen_b10()
    done_c = done_b = False
    while not (done_c and done_b):
        if not done_b:
            done_b = next(gb_, _STOP) is _STOP
        for _ in range(3):
            if not done_c:
                done_c = next(gc_, _STOP) is _STOP
    P.barrier()

    ldw = P.dma("y", WDT.ap().rearrange("p (a b) -> p a b", b=D), wdT_h.ap(),
                ch="de")
    P.wait("v", "de", ldw)
    wsc = P.op("v", lambda v: v.tensor_scalar(
        WDT.ap(), WDT.ap(), SCAL[:, 6:7], None, ALU.mult), inc="vs")
    for tg in range(NTOK // 512):
        P.wait("y", "tv")
        lo = P.dma("y", OBUF.ap().rearrange("p (a b) -> p a b", b=512),
                   o_h.ap()[:, :, 512 * tg:512 * (tg + 1)], ch="da")
        for tt in range(4):
            tglob = 4 * tg + tt
            P.wait("y", "ev")
            lxch = "db" if tt % 2 == 0 else "dc"
            lx = P.dma("y", XRES[tt % 2].ap(),
                       xf_d.ap()[128 * tglob:128 * (tglob + 1), :], ch=lxch)
            for eh in range(EH):
                bi = 2 * (tt % 2) + eh
                bank = PB[bi]
                use_bank(bi)
                P.wait("t", "da", lo)
                P.wait("t", "de", ldw)
                P.wait("t", "vs", wsc)
                for dtile in range(DT):
                    P.op("t", lambda te, b=bank, d2=dtile, tglob=tglob,
                         eh=eh: te.matmul(
                             b.ap()[:, 0:EW],
                             OT3[:, d2 * NTOK + 128 * tglob:
                                 d2 * NTOK + 128 * (tglob + 1)],
                             WOT[:, d2 * D + 512 * eh:
                                 d2 * D + 512 * eh + EW],
                             start=(d2 == 0), stop=False), inc=None)
                for itile in range(IT):
                    P.op("t", lambda te, b=bank, it2=itile, tt=tt, eh=eh:
                         te.matmul(
                             b.ap()[:, 0:EW],
                             OBUF[:, it2 * 512 + 128 * tt:
                                  it2 * 512 + 128 * (tt + 1)],
                             WDT[:, it2 * D + 512 * eh:
                                 512 * eh + it2 * D + EW],
                             start=False, stop=(it2 == IT - 1)),
                         inc="tv" if itile == IT - 1 else None)
                P.wait("v", "tv")
                P.wait("v", lxch, lx)
                P.wait("v", "dd")
                evv = P.op("v", lambda v, b=bank, tt=tt, eh=eh:
                           v.scalar_tensor_tensor(
                               OUTT[tt % 2][:, 512 * eh:512 * eh + EW],
                               b.ap()[:, 0:EW], 1.0,
                               XRES[tt % 2][:, 512 * eh:512 * eh + EW],
                               ALU.mult, ALU.add), inc="ev")
                bank_free[bi] = ("ev", evv)
            P.wait("y", "ev")
            P.dma("y", out_d.ap()[128 * tglob:128 * (tglob + 1), :],
                  OUTT[tt % 2].ap(), ch="dd")
    for chf in ("da", "db", "dc", "dd", "de", "df", "dg"):
        P.wait("y", chf)
    P.barrier()
    return nc


_DIMS_FULL = (1024, 3072, 2048, 4)


def kernel(L0_attn_q, L0_attn_wq, L0_attn_wk, L0_attn_wv, L0_attn_wo,
           L0_ffn_w_gate, L0_ffn_w_up, L0_ffn_w_down):
    D_, I3_, NTOK, n_group = _DIMS_FULL
    NSEQ = NTOK * n_group
    NVT = NSEQ // D_
    NV = NVT // n_group
    x = np.ascontiguousarray(np.asarray(L0_attn_q, dtype=np.float32))
    B = x.shape[0]
    def t_slab(w, nt):
        w = np.asarray(w, dtype=np.float32)
        return np.ascontiguousarray(
            w.T.reshape(nt, 128, -1).transpose(1, 0, 2).reshape(128, -1))

    ws = {"wq": t_slab(L0_attn_wq, 8), "wk": t_slab(L0_attn_wk, 8),
          "wv": t_slab(L0_attn_wv, 8), "wo": t_slab(L0_attn_wo, 8)}
    ws.update({k: np.ascontiguousarray(np.asarray(v, dtype=np.float32))
               for k, v in (("wg", L0_ffn_w_gate), ("wu", L0_ffn_w_up),
                            ("wd", L0_ffn_w_down))})
    idb = np.eye(128, dtype=ml_dtypes.bfloat16)
    nc = build_nc(D_, I3_, NTOK, n_group)
    in_maps = []
    for core in range(2 * n_group):
        b, cp = core // n_group, core % n_group
        xq = np.ascontiguousarray(x[b, NTOK * cp:NTOK * (cp + 1)])
        xf = np.ascontiguousarray(
            x[b].reshape(D_, NVT, D_)[:, NV * cp:NV * (cp + 1), :]
            .reshape(NTOK, D_))
        m = {"xq": xq, "xf": xf, "idb": idb}
        m.update(ws)
        in_maps.append(m)
    res = run_bass_kernel_spmd(nc, in_maps, core_ids=list(range(2 * n_group)))
    out = np.empty((B, NSEQ, D_), dtype=np.float32)
    for core in range(2 * n_group):
        b, cp = core // n_group, core % n_group
        o = np.asarray(res.results[core]["out"])
        out[b].reshape(D_, NVT, D_)[:, NV * cp:NV * (cp + 1), :] = \
            o.reshape(D_, NV, D_)
    return out


# version 12 (final)
# speedup vs baseline: 1.0142x; 1.0142x over previous
import numpy as np
import ml_dtypes
import concourse.bass as bass
import concourse.mybir as mybir
from concourse.bass_utils import run_bass_kernel_spmd

F32 = mybir.dt.float32
F16 = mybir.dt.float16
BF16 = mybir.dt.bfloat16
I32 = mybir.dt.int32
ALU = mybir.AluOpType
AFT = mybir.ActivationFunctionType
AX = mybir.AxisListType

MAGIC = 12582912.0
INV_QA = 1.0 / 32767.0
EPS_LN = 1e-5
MSLACK = 3.0
ENGS = ("t", "v", "s", "g", "y")
SBUF_BASE = 17408
SBUF_CAP = 229200


class Prog:
    def __init__(self, nc, sem_names):
        self.nc = nc
        self.streams = {e: [] for e in ENGS}
        self.counts = {k: 0 for k in sem_names}
        self.water = {}
        self.handles = {}
        for s in sem_names:
            cm = nc.semaphore(s)
            self.handles[s] = cm.__enter__()

    def op(self, eng, maker, inc=None, n=1):
        if inc is not None:
            self.counts[inc] += n
        h = self.handles[inc] if inc is not None else None

        def thunk(e, maker=maker, h=h, n=n):
            ins = maker(e)
            if h is not None:
                ins.then_inc(h, n)
        self.streams[eng].append(thunk)
        return self.counts[inc] if inc is not None else None

    def dma(self, eng, out, in_, ch="da"):
        self.wait(eng, ch, self.counts[ch])
        return self.op(eng, lambda e, o=out, i=in_: e.dma_start(out=o, in_=i),
                       inc=ch, n=16)

    def wait(self, eng, sem, val=None):
        if val is None:
            val = self.counts[sem]
        if val <= 0:
            return
        key = (eng, sem)
        if self.water.get(key, -1) >= val:
            return
        self.water[key] = val
        h = self.handles[sem]
        self.streams[eng].append(lambda e, h=h, v=val: e.wait_ge(h, v))

    DMA_CHANNELS = ("da", "db", "dc", "dd", "de", "df", "dg")

    def barrier(self):
        for ch in self.DMA_CHANNELS:
            self.wait("y", ch)
        streams, self.streams = self.streams, {e: [] for e in ENGS}
        if not any(streams.values()):
            return
        with self.nc.Block() as block:
            for ename, deco in (("t", block.tensor), ("v", block.vector),
                                ("s", block.scalar), ("g", block.gpsimd),
                                ("y", block.sync)):
                th = streams[ename]
                if th:
                    def body(eng, th=th):
                        for f in th:
                            f(eng)
                    deco(body)
        for k in self.counts:
            for e in ENGS:
                self.water[(e, k)] = self.counts[k]


def build_nc(D, I3, NTOK, n_group):
    NSEQ = NTOK * n_group
    DT = D // 128
    IT = I3 // 128
    TT = NTOK // 128
    TCH = NTOK // 512
    NCH = NTOK // 512
    JT = NSEQ // 128
    JCH = NSEQ // 512
    EH = max(D // 512, 1)
    EW = min(D, 512)
    KVK = DT * 128 * NTOK
    KVV = NTOK * D
    NVT = NSEQ // D
    NV = NVT // n_group
    assert NV >= 1 and NVT * D == NSEQ

    nc = bass.Bass("TRN2", target_bir_lowering=False)

    xq_d = nc.dram_tensor("xq", [NTOK, D], F32, kind="ExternalInput")
    xf_d = nc.dram_tensor("xf", [NTOK, D], F32, kind="ExternalInput")
    wT_d = {n: nc.dram_tensor(n, [128, DT * D], F32, kind="ExternalInput")
            for n in ("wq", "wk", "wv", "wo")}
    wg_d = nc.dram_tensor("wg", [I3, D], F32, kind="ExternalInput")
    wu_d = nc.dram_tensor("wu", [I3, D], F32, kind="ExternalInput")
    wd_d = nc.dram_tensor("wd", [D, I3], F32, kind="ExternalInput")
    idb_d = nc.dram_tensor("idb", [128, 128], BF16, kind="ExternalInput")
    out_d = nc.dram_tensor("out", [NTOK, D], F32, kind="ExternalOutput")

    xnf_h = nc.dram_tensor("xnf_h", [TT, 128, D], F32)
    q_h = nc.dram_tensor("q_h", [128, DT, NTOK], BF16)
    kvin_h = nc.dram_tensor("kvin_h", [KVK + KVV], BF16)
    kvout_h = nc.dram_tensor("kvout_h", [n_group, KVK + KVV], BF16)
    wuT_h = nc.dram_tensor("wuT_h", [128, DT, I3], BF16)
    wgT_h = nc.dram_tensor("wgT_h", [128, DT, I3], BF16)
    wdT_h = nc.dram_tensor("wdT_h", [128, IT, D], BF16)
    up_h = nc.dram_tensor("up_h", [IT, 128, NTOK], F16)
    ga_h = nc.dram_tensor("ga_h", [IT, 128, NTOK], F16)
    p_h = nc.dram_tensor("p_h", [IT, 128, NTOK], F16)
    q2_h = nc.dram_tensor("q2_h", [IT, 128, NTOK], F16)
    o_h = nc.dram_tensor("o_h", [128, IT, NTOK], BF16)
    cc_in = [nc.dram_tensor(f"cc{i}_in", [8], F32) for i in range(4)]
    cc_out = [nc.dram_tensor(f"cc{i}_out", [8], F32) for i in range(4)]
    pb_h = nc.dram_tensor("pb_h", [128], F32)
    sb_h = nc.dram_tensor("sb_h", [64], F32)

    def sba(name, shape, dt, off):
        off = off + SBUF_BASE
        itemsz = {F32: 4, F16: 2, BF16: 2, I32: 4}[dt]
        end = off + shape[-1] * itemsz
        assert end <= SBUF_CAP, f"{name}: ends {end} > {SBUF_CAP}"
        return nc.alloc_sbuf_tensor_at(name, shape, dt, offset=off)

    MG = sba("MG", [128, 1], F32, 0)
    EPSB = sba("EPSB", [128, 1], F32, 32)
    ONESB = sba("ONESB", [128, 1], BF16, 64)
    ONE1B = sba("ONE1B", [1, 128], BF16, 96)
    IDB = sba("IDB", [128, 128], BF16, 352)
    STAT = sba("STAT", [128, 16], F32, 608)
    MAXC = sba("MAXC", [128, 32], F32, 672)
    SCAL = sba("SCAL", [128, 16], F32, 800)
    SC1 = sba("SC1", [1, 32], F32, 864)
    RECS = sba("RECS", [128, 4], F32, 992)
    SUMS = sba("SUMS", [128, 4], F32, 1024)
    MXR = sba("MXR", [1, 128], F32, 1056)
    X2S = sba("X2S", [128, 1024], F32, 2048)
    QAR = sba("QAR", [1, NTOK], BF16, 6144)
    ch0 = 17408
    CW = NTOK // 2
    CIN = sba("CIN", [128, CW], F16, ch0)
    CF1 = sba("CF1", [128, CW], F32, ch0 + 2 * CW)
    CI1 = sba("CI1", [128, CW], I32, ch0 + 6 * CW)
    CF2 = sba("CF2", [128, CW], F32, ch0 + 10 * CW)
    CI2 = sba("CI2", [128, CW], I32, ch0 + 14 * CW)
    CO16 = sba("CO16", [128, CW], F16, ch0 + 18 * CW)
    CIN2 = sba("CIN2", [128, CW], F16, ch0 + 20 * CW)
    OBB = sba("OBB", [128, CW], BF16, ch0 + 10 * CW)
    QCB = sba("QCB", [128, DT * 512], BF16, ch0 + 22 * CW)
    ch_end = ch0 + 22 * NTOK
    XNQB = sba("XNQB", [128, D], BF16, ch0)
    XTL = [sba(f"XTL{i}", [128, D], F32, ch0 + 2 * D + i * 4 * D)
           for i in range(2)]
    XFW = [sba(f"XFW{i}", [128, D], F32, ch0 + 10 * D + i * 4 * D)
           for i in range(2)]
    MRW = sba("MRW", [1, NTOK], F32, ch0 + 10 * D)
    QSB = sba("QSB", [128, 512], BF16, ch0)
    QSQ = sba("QSQ", [128, 512], BF16, ch0 + 1024)
    KSQ = sba("KSQ", [128, 512], BF16, ch0 + 2048)
    VST = [sba(f"VST{i}", [128, 512], BF16, ch0 + 3072 + i * 1024)
           for i in range(2)]
    UST16 = [sba(f"UST16_{i}", [128, 512], F16, ch0 + i * 1024)
             for i in range(2)]
    DQmax = max(D, I3)
    DQF = [sba(f"DQF{i}", [128, DQmax], F32, ch0 + 20 * CW + i * 4 * DQmax)
           for i in range(2)]
    QI5 = sba("QI5", [128, D], I32, ch_end - 4 * DQmax)
    OUTT = [sba(f"OUTT{i}", [128, D], F32, ch0 + 2 * D + i * 4 * D)
            for i in range(2)]
    assert ch_end - 4 * DQmax >= ch0 + 18 * D, "B5 QI5/XFW overlap"
    assert 6144 + 2 * NTOK <= 16384
    a0 = ch_end
    XNT = sba("XNT", [128, DT * NTOK], BF16, a0)
    XQT = sba("XQT", [128, DT * NTOK], BF16, a0 + 2 * DT * NTOK)
    ATT = sba("ATT", [128, JT * 512], BF16, a0)
    WDT = sba("WDT", [128, IT * D], BF16, a0)
    WBFX = sba("WBFX", [128, DQmax], BF16, a0)
    WTS2 = [sba(f"WTS2_{i}", [128, 1024], BF16, a0 + 2 * DQmax + i * 2048)
            for i in range(2)]
    DQI = sba("DQI", [128, DQmax], I32, a0 + 2 * DQmax + 4096)
    qn2_off = a0 + 2 * DT * NTOK
    QN2 = sba("QN2", [1, NTOK], F32, qn2_off)
    assert 2 * DQmax + 4096 + 4 * DQmax <= 2 * DT * NTOK, "DQI overflow"
    cur = a0 + max(4 * DT * NTOK, JT * 1024, 2 * IT * D,
                   qn2_off - a0 + 4 * NTOK)
    b0 = cur
    KTL = sba("KTL", [128, DT * NTOK], BF16, b0)
    WT2 = [sba(f"WT2_{i}", [128, DT * D], BF16,
               b0 + 2 * DT * NTOK + i * 2 * DT * D) for i in range(2)]
    CSTG = [sba(f"CSTG{i}", [128, 2048], F32, b0 + i * 8192)
            for i in range(2)]
    KCB = [sba(f"KCB{i}", [128, DT * 512], BF16, b0 + i * DT * 1024)
           for i in range(2)]
    VCB = [sba(f"VCB{i}", [128, 4 * D], BF16, b0 + 2 * DT * 1024 + i * 8 * D)
           for i in range(2)]
    OT3 = sba("OT3", [128, DT * NTOK], BF16,
              b0 + 2 * DT * 1024 + 16 * D)
    OBUF = sba("OBUF", [128, IT * 512], BF16, b0)
    XRES = [sba(f"XRES{i}", [128, D], F32, b0 + IT * 1024 + i * 4 * D)
            for i in range(2)]
    assert IT * 1024 + 8 * D <= 2 * DT * 1024 + 16 * D, \
        "OBUF/XRES overlap OT3"
    cur = b0 + max(2 * DT * NTOK + 4 * DT * D,
                   2 * DT * 1024 + 16 * D + 2 * DT * NTOK)
    WOT = sba("WOT", [128, DT * D], BF16, cur)
    cur += 2 * DT * D
    assert cur <= SBUF_CAP, f"SBUF overflow: {cur}"

    PT = [nc.alloc_psum_tensor(f"PT{i}", [128, 128], BF16) for i in range(2)]
    PB = [nc.alloc_psum_tensor(f"PB{i}", [128, 512], F32) for i in range(6)]

    sems = ["tv", "vt", "sv", "vs", "ts", "st", "gv", "cc", "vv", "pv",
            "ev", "sm", "da", "db", "dc", "dd", "de", "df", "dg"]
    P = Prog(nc, sems)
    bounce_n = [0]

    def bcast(col, dcol):
        i = bounce_n[0]
        bounce_n[0] += 1
        assert i < 64
        P.wait("y", "vv")
        P.dma("y", sb_h[i:i + 1].unsqueeze(0), SC1[0:1, col:col + 1], ch="dg")
        P.wait("y", "dg")
        P.dma("y", SCAL[:, dcol:dcol + 1],
              sb_h[i:i + 1].unsqueeze(0).broadcast_to((128, 1)), ch="dg")

    def partmax_to_sc1(src_col, dst_col):
        P.wait("y", "vs")
        P.dma("y", pb_h.ap(), MAXC[:, src_col:src_col + 1], ch="dg")
        P.wait("y", "dg")
        P.dma("y", MXR.ap(), pb_h.ap().unsqueeze(0), ch="dg")
        P.wait("v", "dg")
        P.op("v", lambda v: v.tensor_reduce(
            SC1[0:1, dst_col:dst_col + 1], MXR.ap(), AX.X, ALU.max),
            inc="vv")

    def transpose_seq(src_getter, nblk, dst_writer):
        free_at = {}
        for i in range(nblk):
            pt = PT[i % 2]
            if i < 2:
                P.wait("t", "vt")
            if i >= 2:
                P.wait("t", "vt", free_at[i - 2])
            tk = P.op("t", lambda te, pt=pt, i=i: te.transpose(
                pt.ap(), src_getter(i), IDB.ap()), inc="tv")
            P.wait("v", "tv", tk)
            dst_writer(i, pt.ap())
            free_at[i] = P.counts["vt"]

    P.op("g", lambda g: g.memset(MG.ap(), MAGIC))
    P.op("g", lambda g: g.memset(EPSB.ap(), EPS_LN))
    P.op("g", lambda g: g.memset(ONESB.ap(), 1.0))
    P.op("g", lambda g: g.memset(ONE1B.ap(), 1.0))
    P.op("g", lambda g: g.memset(MAXC.ap(), 0.0))
    P.op("g", lambda g: g.memset(SC1.ap(), 0.0), inc="gv")
    P.dma("y", IDB.ap(), idb_d.ap(), ch="da")
    P.barrier()

    cast_done = {}

    def load_cast_w(src_d, dst, width, bufs, chw, chans):
        for i in range(width // chw):
            buf = bufs[i % 2]
            ch = chans[i % 2]
            if id(buf) in cast_done:
                P.wait("y", "sv", cast_done[id(buf)])
            ld = P.dma("y", buf.ap()[:, 0:chw],
                       src_d.ap()[:, chw * i:chw * (i + 1)], ch=ch)
            P.wait("s", ch, ld)
            cast_done[id(buf)] = P.op("s", lambda s, buf=buf, i=i, chw=chw,
                                      dst=dst: s.activation(
                dst[:, chw * i:chw * (i + 1)], buf.ap()[:, 0:chw],
                AFT.Identity), inc="sv")

    load_cast_w(wT_d["wq"], WT2[0], DT * D, CSTG, 2048, ("de", "df"))
    load_cast_w(wT_d["wk"], WT2[1], DT * D, CSTG, 2048, ("de", "df"))

    plan = [("q", t) for t in range(TT)] + [("f", t) for t in range(TT)]
    tile_done = {}
    fstore = {}
    for i, (kind, t) in enumerate(plan):
        buf = XTL[i % 2]
        src = xq_d if kind == "q" else xf_d
        if i >= 2:
            P.wait("y", "sv", tile_done[i - 2])
        lch = "da" if i % 2 == 0 else "db"
        ld = P.dma("y", buf.ap(), src.ap()[128 * t:128 * (t + 1), :], ch=lch)
        c0 = 8 * (i % 2)
        P.wait("v", lch, ld)
        P.op("v", lambda v, b=buf, c=c0: v.tensor_reduce(
            STAT[:, c:c + 1], b.ap(), AX.X, ALU.add), inc="vs")
        P.wait("s", lch, ld)
        P.wait("s", "vs")
        s1 = P.op("s", lambda s, b=buf: s.activation(
            X2S[:, 0:D], b.ap(), AFT.Square), inc="sv")
        P.wait("v", "sv", s1)
        P.op("v", lambda v, c=c0: v.tensor_reduce(
            STAT[:, c + 1:c + 2], X2S[:, 0:D], AX.X, ALU.add), inc="vs")
        P.wait("v", "vs")
        P.op("v", lambda v, c=c0: v.tensor_scalar(
            STAT[:, c + 2:c + 3], STAT[:, c:c + 1], 1.0 / D, None, ALU.mult),
            inc="vv")
        P.wait("v", "vv")
        P.op("v", lambda v, c=c0: v.tensor_tensor(
            STAT[:, c + 3:c + 4], STAT[:, c + 2:c + 3], STAT[:, c + 2:c + 3],
            ALU.mult), inc="vv")
        P.wait("v", "vv")
        v4 = P.op("v", lambda v, c=c0: v.scalar_tensor_tensor(
            STAT[:, c + 4:c + 5], STAT[:, c + 1:c + 2], 1.0 / D,
            STAT[:, c + 3:c + 4], ALU.mult, ALU.subtract), inc="vs")
        P.wait("s", "vs", v4)
        s2 = P.op("s", lambda s, c=c0: s.activation(
            STAT[:, c + 5:c + 6], STAT[:, c + 4:c + 5], AFT.Sqrt,
            bias=EPSB.ap()), inc="sv")
        P.wait("v", "sv", s2)
        P.op("v", lambda v, c=c0: v.reciprocal(
            STAT[:, c + 6:c + 7], STAT[:, c + 5:c + 6]), inc="vv")
        P.wait("v", "vv")
        v5 = P.op("v", lambda v, c=c0: v.scalar_tensor_tensor(
            STAT[:, c + 7:c + 8], STAT[:, c + 2:c + 3], -1.0,
            STAT[:, c + 6:c + 7], ALU.mult, ALU.mult), inc="vs")
        P.wait("s", "vs", v5)
        dstb = XNQB if kind == "q" else XFW[t % 2]
        if kind == "f" and t >= 2:
            P.wait("s", "dc" if (t - 2) % 2 == 0 else "dd", fstore[t - 2])
        if kind == "q" and i >= 1:
            P.wait("s", "vt")
        s3 = P.op("s", lambda s, b=buf, db=dstb, c=c0: s.activation(
            db.ap(), b.ap(), AFT.Identity, bias=STAT[:, c + 7:c + 8],
            scale=STAT[:, c + 6:c + 7]), inc="sv")
        tile_done[i] = s3
        if kind == "f":
            P.wait("v", "sv", s3)
            P.op("v", lambda v, db=dstb, t=t: v.tensor_reduce(
                MAXC[:, t:t + 1], db.ap(), AX.X, ALU.max,
                apply_absolute_value=True), inc="vs")
            P.wait("y", "sv", s3)
            fstore[t] = P.dma("y", xnf_h.ap()[t], dstb.ap(),
                              ch="dc" if t % 2 == 0 else "dd")
        else:
            P.wait("t", "sv", s3)

            def wr(dtile, pap, t=t):
                P.op("v", lambda v, pap=pap, dtile=dtile, t=t: v.tensor_copy(
                    XNT[:, dtile * NTOK + 128 * t:
                        dtile * NTOK + 128 * (t + 1)], pap), inc="vt")
            transpose_seq(lambda dtile: XNQB[:, 128 * dtile:
                                             128 * (dtile + 1)], DT, wr)
    P.barrier()

    P.barrier()

    bank_free = {}

    def use_bank(bi, eng="t"):
        if bi in bank_free:
            sem, cnt = bank_free[bi]
            P.wait(eng, sem, cnt)

    def proj_fm(wT, ch, et, bank, inc):
        for dtile in range(DT):
            P.op("t", lambda te, b=bank, d2=dtile, et=et, ch=ch, wT=wT:
                 te.matmul(b.ap(),
                           wT[:, d2 * D + 128 * et:d2 * D + 128 * (et + 1)],
                           XNT[:, d2 * NTOK + 512 * ch:
                               d2 * NTOK + 512 * (ch + 1)],
                           start=(d2 == 0), stop=(d2 == DT - 1)),
                 inc=inc if dtile == DT - 1 else None)
        return P.counts[inc]

    for ch in range(TCH):
        for et in range(DT):
            bank = PB[et % 2]
            use_bank(et % 2)
            mmq = proj_fm(WT2[0], ch, et, bank, "tv")
            P.wait("v", "tv", mmq)
            P.wait("v", "dc")
            P.wait("v", "ts")
            evq = P.op("v", lambda v, b=bank: v.tensor_scalar(
                QSB.ap(), b.ap(), 0.125, None, ALU.mult), inc="vt")
            bank_free[et % 2] = ("vt", evq)
            P.wait("s", "vt", evq)
            sq = P.op("s", lambda s: s.activation(
                QSQ.ap(), QSB.ap(), AFT.Square), inc="sv")
            P.wait("y", "vt", evq)
            P.dma("y", q_h.ap()[:, et, 512 * ch:512 * (ch + 1)], QSB.ap(),
                  ch="dc")
            P.wait("t", "sv", sq)
            use_bank(2)
            nm = P.op("t", lambda te, et=et: te.matmul(
                PB[2][0:1, :], ONESB.ap(), QSQ.ap(),
                start=(et == 0), stop=(et == DT - 1)), inc="ts")
        P.wait("v", "ts")
        qe = P.op("v", lambda v, ch=ch: v.tensor_copy(
            QN2[0:1, 512 * ch:512 * (ch + 1)], PB[2][0:1, :]), inc="vt")
        bank_free[2] = ("vt", qe)
    P.barrier()
    for ch in range(TCH):
        for et in range(DT):
            bank = PB[et % 2]
            use_bank(et % 2)
            mmk = proj_fm(WT2[1], ch, et, bank, "tv")
            P.wait("v", "tv", mmk)
            evk = P.op("v", lambda v, b=bank, et=et, ch=ch: v.tensor_copy(
                KTL[:, et * NTOK + 512 * ch:et * NTOK + 512 * (ch + 1)],
                b.ap()), inc="vt")
            bank_free[et % 2] = ("vt", evk)
    P.wait("s", "vt")
    for ch in range(TCH):
        for et in range(DT):
            P.wait("s", "ts")
            sq = P.op("s", lambda s, et=et, ch=ch: s.activation(
                KSQ.ap(), KTL[:, et * NTOK + 512 * ch:
                              et * NTOK + 512 * (ch + 1)], AFT.Square),
                inc="sv")
            P.wait("t", "sv", sq)
            use_bank(2)
            P.op("t", lambda te, et=et: te.matmul(
                PB[2][0:1, :], ONESB.ap(), KSQ.ap(),
                start=(et == 0), stop=(et == DT - 1)), inc="ts")
        P.wait("v", "ts")
        ke = P.op("v", lambda v, ch=ch: v.tensor_copy(
            MRW[0:1, 512 * ch:512 * (ch + 1)], PB[2][0:1, :]), inc="vt")
        bank_free[2] = ("vt", ke)
    P.wait("v", "vt")
    P.op("v", lambda v: v.tensor_reduce(
        SC1[0:1, 1:2], MRW.ap(), AX.X, ALU.max), inc="vv")
    P.wait("y", "vt")
    P.dma("y", kvin_h[0:KVK].rearrange("(p a b) -> p a b", p=128, a=DT),
          KTL.ap().rearrange("p (a b) -> p a b", b=NTOK), ch="de")
    P.barrier()
    load_cast_w(wT_d["wv"], WT2[0], DT * D, XTL, 1024, ("da", "db"))
    P.barrier()
    for jt in range(TT):
        for cch in range(EH):
            k = jt * EH + cch
            bank = PB[k % 2]
            use_bank(k % 2)
            for dtile in range(DT):
                P.op("t", lambda te, b=bank, d2=dtile, jt=jt, cch=cch:
                     te.matmul(b.ap()[:, 0:EW],
                               XNT[:, d2 * NTOK + 128 * jt:
                                   d2 * NTOK + 128 * (jt + 1)],
                               WT2[0][:, d2 * D + 512 * cch:
                                      d2 * D + 512 * cch + EW],
                               start=(d2 == 0), stop=(d2 == DT - 1)),
                     inc="tv" if dtile == DT - 1 else None)
            P.wait("v", "tv")
            P.wait("v", "dc" if k % 2 == 0 else "dd")
            ev = P.op("v", lambda v, b=bank, k=k: v.tensor_copy(
                VST[k % 2].ap()[:, 0:EW], b.ap()[:, 0:EW]), inc="vt")
            bank_free[k % 2] = ("vt", ev)
            P.wait("y", "vt", ev)
            P.dma("y", kvin_h[KVK:KVK + KVV].rearrange(
                "(p a b) -> p a b", p=128, a=TT)[:, jt,
                                                 512 * cch:512 * cch + EW],
                  VST[k % 2].ap()[:, 0:EW],
                  ch="dc" if k % 2 == 0 else "dd")
    P.wait("v", "vs")
    P.op("v", lambda v: v.tensor_reduce(
        MAXC[:, 30:31], MAXC[:, 0:TT], AX.X, ALU.max), inc="vs")
    partmax_to_sc1(30, 0)
    P.wait("y", "vv")
    P.dma("y", cc_in[0].ap().unsqueeze(0), SC1[0:1, 0:8], ch="df")
    P.barrier()

    ncores = 2 * n_group
    groups8 = [list(range(ncores))]
    kvgroups = [list(range(n_group)), list(range(n_group, ncores))]
    P.wait("g", "df")
    P.wait("g", "de")
    P.wait("g", "dc")
    P.wait("g", "dd")
    cc1n = P.op("g", lambda g: g.collective_compute(
        "AllReduce", ALU.max, replica_groups=groups8,
        ins=[cc_in[0].ap().opt()], outs=[cc_out[0].ap().opt()]), inc="cc")
    P.op("g", lambda g: g.collective_compute(
        "AllGather", ALU.bypass, replica_groups=kvgroups,
        ins=[kvin_h.ap().opt()], outs=[kvout_h.ap().opt()]), inc="cc")

    dqf_done = {}

    def prep_ffn_w(src_d, rows, rowlen, dstT_h):
        nb = rowlen // 32
        for rt in range(rows):
            buf = DQF[rt % 2]
            wch = "da" if rt % 2 == 0 else "db"
            if id(buf) in dqf_done:
                P.wait("y", "vs", dqf_done[id(buf)])
            ld = P.dma("y", buf[:, 0:rowlen],
                       src_d.ap()[128 * rt:128 * (rt + 1), :], ch=wch)
            P.wait("v", wch, ld)
            r1 = P.op("v", lambda v, nb=nb, rl=rowlen, buf=buf: v.tensor_reduce(
                X2S[:, 0:nb],
                buf[:, 0:rl].rearrange("p (a b) -> p a b", b=32),
                AX.X, ALU.max, apply_absolute_value=True), inc="vv")
            P.wait("v", "vv", r1)
            P.op("v", lambda v, nb=nb: v.reciprocal(
                X2S[:, nb:2 * nb], X2S[:, 0:nb]), inc="vv")
            P.wait("v", "vv")
            P.op("v", lambda v, nb=nb: v.tensor_scalar(
                X2S[:, nb:2 * nb], X2S[:, nb:2 * nb], 7.0, 1e12,
                ALU.mult, ALU.min), inc="vv")
            P.op("v", lambda v, nb=nb: v.tensor_scalar(
                X2S[:, 2 * nb:3 * nb], X2S[:, 0:nb], 1.0 / 7.0, 1e-12,
                ALU.mult, ALU.max), inc="vv")
            P.wait("v", "vv")
            P.op("v", lambda v, nb=nb, rl=rowlen, buf=buf: v.tensor_tensor(
                buf[:, 0:rl].rearrange("p (a b) -> p a b", b=32),
                buf[:, 0:rl].rearrange("p (a b) -> p a b", b=32),
                X2S[:, nb:2 * nb].unsqueeze(-1).broadcast_to((128, nb, 32)),
                ALU.mult), inc="vv")
            P.wait("v", "vv")
            P.op("v", lambda v, rl=rowlen, buf=buf: v.tensor_scalar(
                DQI[:, 0:rl], buf[:, 0:rl], MAGIC, MAGIC,
                ALU.add, ALU.subtract), inc="vv")
            P.wait("v", "vv")
            P.wait("v", "tv")
            cv = P.op("v", lambda v, nb=nb, rl=rowlen: v.tensor_tensor(
                WBFX[:, 0:rl].rearrange("p (a b) -> p a b", b=32),
                DQI[:, 0:rl].rearrange("p (a b) -> p a b", b=32),
                X2S[:, 2 * nb:3 * nb].unsqueeze(-1).broadcast_to(
                    (128, nb, 32)), ALU.mult), inc="vs")
            dqf_done[id(buf)] = cv
            P.wait("t", "vs", cv)
            nblk = rowlen // 128

            def wr(bi, pap, rt=rt, dstT_h=dstT_h, nblk=nblk):
                wts = WTS2[(bi // 8) % 2]
                if bi % 8 == 0:
                    P.wait("v", "dc" if (bi // 8) % 2 == 0 else "dd")
                P.op("v", lambda v, pap=pap, bi=bi, wts=wts: v.tensor_copy(
                    wts[:, 128 * (bi % 8):128 * (bi % 8 + 1)], pap),
                    inc="vt")
                if bi % 8 == 7 or bi == nblk - 1:
                    base = (bi // 8) * 8
                    cnt = bi - base + 1
                    P.wait("y", "vt")
                    P.dma("y", dstT_h.ap()[:, base:base + cnt,
                                           128 * rt:128 * (rt + 1)],
                          wts[:, 0:cnt * 128].rearrange(
                              "p (a b) -> p a b", b=128),
                          ch="dc" if (bi // 8) % 2 == 0 else "dd")
            transpose_seq(lambda bi: WBFX[:, 128 * bi:128 * (bi + 1)],
                          nblk, wr)

    prep_ffn_w(wu_d, IT, D, wuT_h)
    prep_ffn_w(wg_d, IT, D, wgT_h)
    prep_ffn_w(wd_d, DT, I3, wdT_h)
    load_cast_w(wT_d["wo"], WOT, DT * D, XTL, 1024, ("df", "dg"))
    P.wait("y", "cc", cc1n)
    P.dma("y", SC1[0:1, 8:16], cc_out[0].ap().unsqueeze(0), ch="df")
    P.wait("v", "df")
    P.op("v", lambda v: v.tensor_scalar(
        SC1[0:1, 16:17], SC1[0:1, 8:9], INV_QA, 1e-12, ALU.mult, ALU.max),
        inc="vv")
    P.op("v", lambda v: v.reciprocal(SC1[0:1, 17:18], SC1[0:1, 8:9]),
         inc="vv")
    P.wait("v", "vv")
    P.op("v", lambda v: v.tensor_scalar(
        SC1[0:1, 17:18], SC1[0:1, 17:18], 32767.0, 1e12, ALU.mult, ALU.min),
        inc="vv")
    P.wait("v", "vv")
    P.op("v", lambda v: v.tensor_scalar(
        MRW.ap(), QN2.ap(), SC1[0:1, 9:10], None, ALU.mult), inc="vs")
    P.wait("s", "vs")
    P.op("s", lambda s: s.activation(MRW.ap(), MRW.ap(), AFT.Sqrt),
         inc="sv")
    P.wait("v", "sv")
    P.op("v", lambda v: v.tensor_scalar(
        QAR.ap(), MRW.ap(), -1.0, -MSLACK, ALU.mult, ALU.add), inc="vv")
    bcast(17, 0)
    bcast(16, 7)
    P.barrier()

    for t in range(TT):
        buf = XFW[t % 2]
        if t >= 2:
            P.wait("y", "sv")
        lch = "da" if t % 2 == 0 else "db"
        ld = P.dma("y", buf.ap(), xnf_h.ap()[t], ch=lch)
        P.wait("s", lch, ld)
        P.wait("s", "vv")
        sa = P.op("s", lambda s, b=buf: s.activation(
            X2S.ap()[:, 0:D], b.ap(), AFT.Identity, bias=MG.ap(),
            scale=SCAL[:, 0:1]), inc="sv")
        P.wait("v", "sv", sa)
        P.wait("v", "vs")
        P.op("v", lambda v: v.tensor_scalar(
            QI5.ap(), X2S.ap()[:, 0:D], MAGIC, None, ALU.subtract),
            inc="vv")
        P.wait("v", "vv")
        P.wait("v", "tv")
        cb = P.op("v", lambda v: v.tensor_copy(XNQB.ap(), QI5.ap()),
                  inc="vs")
        P.wait("t", "vs", cb)

        def wr(dtile, pap, t=t):
            P.op("v", lambda v, pap=pap, dtile=dtile, t=t: v.tensor_copy(
                XQT[:, dtile * NTOK + 128 * t:dtile * NTOK + 128 * (t + 1)],
                pap), inc="vt")
        transpose_seq(lambda dtile: XNQB[:, 128 * dtile:128 * (dtile + 1)],
                      DT, wr)
    P.barrier()

    P.op("g", lambda g: g.memset(MAXC[:, 28:29], 0.0))
    P.op("g", lambda g: g.memset(MAXC[:, 29:30], 0.0), inc="gv")
    P.wait("v", "gv")
    bank_free.clear()
    for wtag, srcT_h, dst_h, mxcol in (("u", wuT_h, up_h, 28),
                                       ("g", wgT_h, ga_h, 29)):
        for it in range(IT):
            wub = WT2[it % 2]
            P.wait("y", "tv")
            wch = "da" if it % 2 == 0 else "db"
            ld = P.dma("y", wub[:, 0:DT * 128].rearrange("p (a b) -> p a b",
                                                         b=128),
                       srcT_h.ap()[:, :, 128 * it:128 * (it + 1)], ch=wch)
            P.wait("t", wch, ld)
            for ch in range(TCH):
                k = it * TCH + ch
                bank = PB[k % 2]
                use_bank(k % 2)
                for ht in range(DT):
                    P.op("t", lambda te, b=bank, ht=ht, ch=ch, wub=wub:
                         te.matmul(b.ap(),
                                   wub[:, 128 * ht:128 * (ht + 1)],
                                   XQT[:, ht * NTOK + 512 * ch:
                                       ht * NTOK + 512 * (ch + 1)],
                                   start=(ht == 0), stop=(ht == DT - 1)),
                         inc="tv" if ht == DT - 1 else None)
                u16 = UST16[k % 2]
                P.wait("v", "tv")
                P.wait("v", "dc" if k % 2 == 0 else "dd")
                ev = P.op("v", lambda v, b=bank, u16=u16: v.tensor_scalar(
                    u16.ap(), b.ap(), SCAL[:, 7:8], None, ALU.mult),
                    inc="vt")
                P.wait("v", "vt", ev)
                mx = P.op("v", lambda v, b=bank: v.tensor_reduce(
                    STAT[:, 8:9], b.ap(), AX.X, ALU.max,
                    apply_absolute_value=True), inc="vv")
                bank_free[k % 2] = ("vv", mx)
                P.wait("v", "vv", mx)
                P.op("v", lambda v, mxcol=mxcol: v.tensor_tensor(
                    MAXC[:, mxcol:mxcol + 1], MAXC[:, mxcol:mxcol + 1],
                    STAT[:, 8:9], ALU.max), inc="vs")
                P.wait("y", "vt", ev)
                P.dma("y", dst_h.ap()[it][:, 512 * ch:512 * (ch + 1)],
                      u16.ap(), ch="dc" if k % 2 == 0 else "dd")
        partmax_to_sc1(mxcol, 2 if wtag == "u" else 3)
    P.wait("y", "vv")
    P.dma("y", cc_in[1].ap().unsqueeze(0), SC1[0:1, 0:8], ch="df")
    P.barrier()
    cc2n = P.op("g", lambda g: g.collective_compute(
        "AllReduce", ALU.max, replica_groups=groups8,
        ins=[cc_in[1].ap().opt()], outs=[cc_out[1].ap().opt()]), inc="cc")
    P.wait("y", "cc", cc2n)
    P.dma("y", SC1[0:1, 8:16], cc_out[1].ap().unsqueeze(0), ch="df")

    def gen_chains():
        P.wait("v", "df")
        P.op("v", lambda v: v.tensor_tensor(
            SC1[0:1, 18:19], SC1[0:1, 16:17], SC1[0:1, 10:11], ALU.mult),
            inc="vv")
        P.op("v", lambda v: v.tensor_tensor(
            SC1[0:1, 19:20], SC1[0:1, 16:17], SC1[0:1, 11:12], ALU.mult),
            inc="vv")
        P.wait("v", "vv")
        P.op("v", lambda v: v.tensor_scalar(
            SC1[0:1, 20:21], SC1[0:1, 18:19], INV_QA, 1e-12, ALU.mult,
            ALU.max), inc="vv")
        P.op("v", lambda v: v.tensor_scalar(
            SC1[0:1, 22:23], SC1[0:1, 19:20], INV_QA, 1e-12, ALU.mult,
            ALU.max), inc="vv")
        P.wait("v", "vv")
        P.op("v", lambda v: v.reciprocal(SC1[0:1, 21:22], SC1[0:1, 20:21]),
             inc="vv")
        P.wait("v", "vv")
        P.op("v", lambda v: v.reciprocal(SC1[0:1, 23:24], SC1[0:1, 22:23]),
             inc="vv")
        P.wait("v", "vv")
        bcast(23, 1)
        bcast(22, 2)
        bcast(21, 3)
        P.op("g", lambda g: g.memset(MAXC[:, 27:28], 0.0), inc="gv")
        P.wait("v", "gv")
        yield
        for it in range(IT):
            for h in range(2):
                sl = slice(CW * h, CW * (h + 1))
                P.wait("y", "vs")
                ld = P.dma("y", CIN.ap(), ga_h.ap()[it][:, sl], ch="df")
                P.wait("v", "df", ld)
                P.op("v", lambda v: v.tensor_scalar(
                    CF1.ap(), CIN.ap(), SCAL[:, 1:2], MAGIC,
                    ALU.mult, ALU.add), inc="vv")
                P.wait("v", "vv")
                c1 = P.op("v", lambda v: v.tensor_scalar(
                    CI1.ap(), CF1.ap(), MAGIC, None, ALU.subtract),
                    inc="vs")
                yield
                P.wait("s", "vs", c1)
                f2 = P.op("s", lambda s: s.activation(
                    CF2.ap(), CI1.ap(), AFT.Sigmoid, scale=SCAL[:, 2:3]),
                    inc="sv")
                P.wait("v", "sv", f2)
                P.op("v", lambda v: v.tensor_scalar(
                    CF1.ap(), CF2.ap(), 65536.0, MAGIC,
                    ALU.mult, ALU.add), inc="vv")
                P.wait("v", "vv")
                P.op("v", lambda v: v.tensor_scalar(
                    CI2.ap(), CF1.ap(), MAGIC, None, ALU.subtract),
                    inc="vv")
                yield
                P.wait("v", "vv")
                P.op("v", lambda v: v.scalar_tensor_tensor(
                    CF2.ap(), CI2.ap(), 65535.0, CI1.ap(), ALU.min,
                    ALU.mult), inc="vv")
                P.wait("v", "vv")
                P.wait("v", "de")
                ev = P.op("v", lambda v: v.tensor_scalar(
                    CO16.ap(), CF2.ap(), 2.0 ** -16, None, ALU.mult),
                    inc="vv")
                P.wait("v", "vv", ev)
                P.op("v", lambda v: v.tensor_reduce(
                    STAT[:, 8:9], CO16.ap(), AX.X, ALU.max,
                    apply_absolute_value=True), inc="vv")
                P.wait("v", "vv")
                vb = P.op("v", lambda v: v.tensor_tensor(
                    MAXC[:, 27:28], MAXC[:, 27:28], STAT[:, 8:9], ALU.max),
                    inc="vs")
                P.wait("y", "vs", vb)
                P.dma("y", p_h.ap()[it][:, sl], CO16.ap(), ch="de")
                yield
        partmax_to_sc1(27, 4)
        P.wait("y", "vv")
        P.dma("y", cc_in[2].ap().unsqueeze(0), SC1[0:1, 0:8], ch="df")
        for _ch in Prog.DMA_CHANNELS:
            P.wait("g", _ch)
        cc3n = P.op("g", lambda g: g.collective_compute(
            "AllReduce", ALU.max, replica_groups=groups8,
            ins=[cc_in[2].ap().opt()], outs=[cc_out[2].ap().opt()]),
            inc="cc")
        P.wait("y", "cc", cc3n)
        P.dma("y", SC1[0:1, 8:16], cc_out[2].ap().unsqueeze(0), ch="df")
        yield
        P.wait("v", "df")
        P.op("v", lambda v: v.tensor_tensor(
            SC1[0:1, 18:19], SC1[0:1, 22:23], SC1[0:1, 12:13], ALU.mult),
            inc="vv")
        P.wait("v", "vv")
        P.op("v", lambda v: v.tensor_scalar(
            SC1[0:1, 26:27], SC1[0:1, 18:19], INV_QA, 1e-12, ALU.mult,
            ALU.max), inc="vv")
        P.wait("v", "vv")
        P.op("v", lambda v: v.reciprocal(SC1[0:1, 27:28], SC1[0:1, 26:27]),
             inc="vv")
        P.wait("v", "vv")
        P.op("v", lambda v: v.tensor_tensor(
            SC1[0:1, 28:29], SC1[0:1, 22:23], SC1[0:1, 27:28], ALU.mult),
            inc="vv")
        bcast(28, 4)
        P.op("g", lambda g: g.memset(MAXC[:, 26:27], 0.0), inc="gv")
        P.wait("v", "gv")
        P.wait("y", "de")
        yield
        for it in range(IT):
            for h in range(2):
                sl = slice(CW * h, CW * (h + 1))
                P.wait("y", "vs")
                l1 = P.dma("y", CIN.ap(), up_h.ap()[it][:, sl], ch="df")
                l2 = P.dma("y", CIN2.ap(), p_h.ap()[it][:, sl], ch="dg")
                P.wait("v", "df", l1)
                P.op("v", lambda v: v.tensor_scalar(
                    CF1.ap(), CIN.ap(), SCAL[:, 3:4], MAGIC,
                    ALU.mult, ALU.add), inc="vv")
                P.wait("v", "vv")
                P.op("v", lambda v: v.tensor_scalar(
                    CI1.ap(), CF1.ap(), MAGIC, None, ALU.subtract),
                    inc="vs")
                yield
                P.wait("v", "dg", l2)
                P.op("v", lambda v: v.tensor_scalar(
                    CF2.ap(), CIN2.ap(), SCAL[:, 4:5], MAGIC,
                    ALU.mult, ALU.add), inc="vv")
                P.wait("v", "vv")
                P.op("v", lambda v: v.tensor_scalar(
                    CI2.ap(), CF2.ap(), MAGIC, None, ALU.subtract),
                    inc="vv")
                yield
                P.wait("v", "vv")
                P.op("v", lambda v: v.tensor_tensor(
                    CF1.ap(), CI1.ap(), CI2.ap(), ALU.mult), inc="vv")
                P.wait("v", "vv")
                P.wait("v", "de")
                ev = P.op("v", lambda v: v.tensor_scalar(
                    CO16.ap(), CF1.ap(), 2.0 ** -15, None, ALU.mult),
                    inc="vv")
                P.wait("v", "vv", ev)
                P.op("v", lambda v: v.tensor_reduce(
                    STAT[:, 8:9], CO16.ap(), AX.X, ALU.max,
                    apply_absolute_value=True), inc="vv")
                P.wait("v", "vv")
                vb = P.op("v", lambda v: v.tensor_tensor(
                    MAXC[:, 26:27], MAXC[:, 26:27], STAT[:, 8:9], ALU.max),
                    inc="vs")
                P.wait("y", "vs", vb)
                P.dma("y", q2_h.ap()[it][:, sl], CO16.ap(), ch="de")
                yield
        partmax_to_sc1(26, 5)
        P.wait("y", "vv")
        P.dma("y", cc_in[3].ap().unsqueeze(0), SC1[0:1, 0:8], ch="df")
        for _ch in Prog.DMA_CHANNELS:
            P.wait("g", _ch)
        cc4n = P.op("g", lambda g: g.collective_compute(
            "AllReduce", ALU.max, replica_groups=groups8,
            ins=[cc_in[3].ap().opt()], outs=[cc_out[3].ap().opt()]),
            inc="cc")
        P.wait("y", "cc", cc4n)
        P.dma("y", SC1[0:1, 8:16], cc_out[3].ap().unsqueeze(0), ch="df")
        yield
        P.wait("v", "df")
        P.op("v", lambda v: v.tensor_tensor(
            SC1[0:1, 18:19], SC1[0:1, 20:21], SC1[0:1, 26:27], ALU.mult),
            inc="vv")
        P.wait("v", "vv")
        P.op("v", lambda v: v.tensor_scalar(
            SC1[0:1, 18:19], SC1[0:1, 18:19], 2.0 ** 15, None, ALU.mult),
            inc="vv")
        P.wait("v", "vv")
        P.op("v", lambda v: v.tensor_tensor(
            SC1[0:1, 19:20], SC1[0:1, 18:19], SC1[0:1, 13:14], ALU.mult),
            inc="vv")
        P.wait("v", "vv")
        P.op("v", lambda v: v.tensor_scalar(
            SC1[0:1, 29:30], SC1[0:1, 19:20], INV_QA, 1e-12, ALU.mult,
            ALU.max), inc="vv")
        P.wait("v", "vv")
        P.op("v", lambda v: v.reciprocal(SC1[0:1, 30:31], SC1[0:1, 29:30]),
             inc="vv")
        P.wait("v", "vv")
        P.op("v", lambda v: v.tensor_tensor(
            SC1[0:1, 31:32], SC1[0:1, 18:19], SC1[0:1, 30:31], ALU.mult),
            inc="vv")
        bcast(31, 5)
        bcast(29, 6)
        P.wait("y", "de")
        yield
        for it in range(IT):
            for h in range(2):
                sl = slice(CW * h, CW * (h + 1))
                P.wait("y", "vv")
                ld = P.dma("y", CIN.ap(), q2_h.ap()[it][:, sl], ch="df")
                P.wait("v", "df", ld)
                P.op("v", lambda v: v.tensor_scalar(
                    CF1.ap(), CIN.ap(), SCAL[:, 5:6], MAGIC,
                    ALU.mult, ALU.add), inc="vv")
                P.wait("v", "vv")
                P.op("v", lambda v: v.tensor_scalar(
                    CI1.ap(), CF1.ap(), MAGIC, None, ALU.subtract),
                    inc="vv")
                yield
                P.wait("v", "vv")
                P.wait("v", "de")
                cb = P.op("v", lambda v: v.tensor_copy(OBB.ap(), CI1.ap()),
                          inc="vv")
                P.wait("y", "vv", cb)
                P.dma("y", o_h.ap()[:, it, sl], OBB.ap(), ch="de")
                yield

    def gen_b10():
        pv_prev = 0
        ts_after = {}
        pv_after = {}
        for qc in range(NCH):
            P.wait("y", "ts")
            lq = P.dma("y", QCB.ap().rearrange("p (a b) -> p a b", b=512),
                       q_h.ap()[:, :, 512 * qc:512 * (qc + 1)], ch="de")
            P.wait("t", "de", lq)
            exp_base = P.counts["st"]
            if qc > 0:
                P.wait("t", "st", exp_base)
                P.wait("t", "sm")
            for jc in range(JCH):
                r = jc // (NTOK // 512)
                c = jc % (NTOK // 512)
                kb = KCB[jc % 2]
                if jc >= 2:
                    P.wait("y", "ts", ts_after[(qc, jc - 2)])
                kch = "da" if jc % 2 == 0 else "db"
                lk = P.dma("y", kb.ap().rearrange("p (a b) -> p a b", b=512),
                           kvout_h.ap()[r][0:KVK].rearrange(
                               "(p a b) -> p a b", p=128, a=DT)[
                               :, :, 512 * c:512 * (c + 1)], ch=kch)
                P.wait("t", kch, lk)
                for j4 in range(4):
                    j = jc * 4 + j4
                    bank = PB[4 + (j % 2)]
                    if j >= 2:
                        P.wait("t", "st", exp_base + j - 1)
                    P.op("t", lambda te, b=bank, qc=qc: te.matmul(
                        b.ap(), ONE1B.ap(),
                        QAR[0:1, 512 * qc:512 * (qc + 1)],
                        start=True, stop=False))
                    for et in range(DT):
                        P.op("t", lambda te, b=bank, kb=kb, et=et, j4=j4:
                             te.matmul(b.ap(),
                                       kb[:, et * 512 + 128 * j4:
                                          et * 512 + 128 * (j4 + 1)],
                                       QCB[:, et * 512:(et + 1) * 512],
                                       start=False, stop=(et == DT - 1)),
                             inc="ts" if et == DT - 1 else None)
                    P.wait("s", "ts")
                    if qc > 0 and j == 0:
                        P.wait("s", "pv", pv_prev)
                    P.op("s", lambda s, b=bank, j=j: s.activation(
                        ATT[:, j * 512:(j + 1) * 512], b.ap(), AFT.Exp),
                        inc="st")
                ts_after[(qc, jc)] = P.counts["ts"]
                yield
            P.wait("t", "st")
            if qc > 0:
                P.wait("t", "ev")
            else:
                P.wait("t", "vv")
                P.wait("t", "vt")
            for half in range(EH):
                if half > 0:
                    P.wait("t", "ev")
                for jc in range(JCH):
                    r = jc // (NTOK // 512)
                    c = jc % (NTOK // 512)
                    vb = VCB[jc % 2]
                    if jc >= 2:
                        P.wait("y", "pv", pv_after[(qc, half, jc - 2)])
                    if jc < 2:
                        P.wait("y", "pv")
                    vch = "dc" if jc % 2 == 0 else "dd"
                    lv = P.dma("y", vb.ap()[:, 0:4 * EW].rearrange(
                        "p (a b) -> p a b", b=EW),
                        kvout_h.ap()[r][KVK:KVK + KVV].rearrange(
                            "(p a b) -> p a b", p=128, a=TT)[
                            :, 4 * c:4 * (c + 1),
                            512 * half:512 * half + EW], ch=vch)
                    P.wait("t", vch, lv)
                    for j4 in range(4):
                        j = jc * 4 + j4
                        for qt in range(4):
                            last = (j4 == 3 and qt == 3)
                            P.op("t", lambda te, j=j, qt=qt, vb=vb, j4=j4:
                                 te.matmul(PB[qt].ap()[:, 0:EW],
                                           ATT[:, j * 512 + 128 * qt:
                                               j * 512 + 128 * (qt + 1)],
                                           vb[:, j4 * EW:(j4 + 1) * EW],
                                           start=(j == 0),
                                           stop=(j == JT - 1)),
                                 inc="pv" if (last and half > 0) else None)
                            if half == 0:
                                P.op("t", lambda te, j=j, qt=qt: te.matmul(
                                    PB[4][:, qt:qt + 1],
                                    ATT[:, j * 512 + 128 * qt:
                                        j * 512 + 128 * (qt + 1)],
                                    ONESB.ap(),
                                    start=(j == 0 and qt == 0),
                                    stop=(j == JT - 1 and qt == 3)),
                                    inc="pv" if last else None)
                    pv_after[(qc, half, jc)] = P.counts["pv"]
                    yield
                P.wait("v", "pv")
                P.wait("s", "pv")
                if half == 0:
                    P.op("v", lambda v: v.tensor_copy(SUMS.ap(),
                                                      PB[4][:, 0:4]),
                         inc="sm")
                    P.wait("v", "sm")
                    P.op("v", lambda v: v.reciprocal(RECS.ap(), SUMS.ap()),
                         inc="sm")
                    P.wait("v", "sm")
                P.wait("s", "sm")
                for qt in range(4):
                    rloc = qc * 512 + qt * 128
                    vblk = rloc // D
                    d0t = (rloc % D) // 128
                    col0 = d0t * NTOK + NV * 512 * half + vblk
                    P.op("s", lambda s, qt=qt, col0=col0:
                         s.activation(
                             OT3[:, col0:col0 + NV * (EW - 1) + 1:NV],
                             PB[qt].ap()[:, 0:EW], AFT.Identity,
                             scale=RECS[:, qt:qt + 1]), inc="ev")
                yield
            pv_prev = P.counts["pv"]

    _STOP = object()
    gc_, gb_ = gen_chains(), gen_b10()
    done_c = done_b = False
    while not (done_c and done_b):
        if not done_b:
            done_b = next(gb_, _STOP) is _STOP
        for _ in range(2):
            if not done_c:
                done_c = next(gc_, _STOP) is _STOP
    P.barrier()

    ldw = P.dma("y", WDT.ap().rearrange("p (a b) -> p a b", b=D), wdT_h.ap(),
                ch="de")
    P.wait("v", "de", ldw)
    wsc = P.op("v", lambda v: v.tensor_scalar(
        WDT.ap(), WDT.ap(), SCAL[:, 6:7], None, ALU.mult), inc="vs")
    for tg in range(NTOK // 512):
        P.wait("y", "tv")
        lo = P.dma("y", OBUF.ap().rearrange("p (a b) -> p a b", b=512),
                   o_h.ap()[:, :, 512 * tg:512 * (tg + 1)], ch="da")
        for tt in range(4):
            tglob = 4 * tg + tt
            P.wait("y", "ev")
            lxch = "db" if tt % 2 == 0 else "dc"
            lx = P.dma("y", XRES[tt % 2].ap(),
                       xf_d.ap()[128 * tglob:128 * (tglob + 1), :], ch=lxch)
            for eh in range(EH):
                bi = 2 * (tt % 2) + eh
                bank = PB[bi]
                use_bank(bi)
                P.wait("t", "da", lo)
                P.wait("t", "de", ldw)
                P.wait("t", "vs", wsc)
                for dtile in range(DT):
                    P.op("t", lambda te, b=bank, d2=dtile, tglob=tglob,
                         eh=eh: te.matmul(
                             b.ap()[:, 0:EW],
                             OT3[:, d2 * NTOK + 128 * tglob:
                                 d2 * NTOK + 128 * (tglob + 1)],
                             WOT[:, d2 * D + 512 * eh:
                                 d2 * D + 512 * eh + EW],
                             start=(d2 == 0), stop=False), inc=None)
                for itile in range(IT):
                    P.op("t", lambda te, b=bank, it2=itile, tt=tt, eh=eh:
                         te.matmul(
                             b.ap()[:, 0:EW],
                             OBUF[:, it2 * 512 + 128 * tt:
                                  it2 * 512 + 128 * (tt + 1)],
                             WDT[:, it2 * D + 512 * eh:
                                 512 * eh + it2 * D + EW],
                             start=False, stop=(it2 == IT - 1)),
                         inc="tv" if itile == IT - 1 else None)
                P.wait("v", "tv")
                P.wait("v", lxch, lx)
                P.wait("v", "dd")
                evv = P.op("v", lambda v, b=bank, tt=tt, eh=eh:
                           v.scalar_tensor_tensor(
                               OUTT[tt % 2][:, 512 * eh:512 * eh + EW],
                               b.ap()[:, 0:EW], 1.0,
                               XRES[tt % 2][:, 512 * eh:512 * eh + EW],
                               ALU.mult, ALU.add), inc="ev")
                bank_free[bi] = ("ev", evv)
            P.wait("y", "ev")
            P.dma("y", out_d.ap()[128 * tglob:128 * (tglob + 1), :],
                  OUTT[tt % 2].ap(), ch="dd")
    for chf in ("da", "db", "dc", "dd", "de", "df", "dg"):
        P.wait("y", chf)
    P.barrier()
    return nc


_DIMS_FULL = (1024, 3072, 2048, 4)


def kernel(L0_attn_q, L0_attn_wq, L0_attn_wk, L0_attn_wv, L0_attn_wo,
           L0_ffn_w_gate, L0_ffn_w_up, L0_ffn_w_down):
    D_, I3_, NTOK, n_group = _DIMS_FULL
    NSEQ = NTOK * n_group
    NVT = NSEQ // D_
    NV = NVT // n_group
    x = np.ascontiguousarray(np.asarray(L0_attn_q, dtype=np.float32))
    B = x.shape[0]
    def t_slab(w, nt):
        w = np.asarray(w, dtype=np.float32)
        return np.ascontiguousarray(
            w.T.reshape(nt, 128, -1).transpose(1, 0, 2).reshape(128, -1))

    ws = {"wq": t_slab(L0_attn_wq, 8), "wk": t_slab(L0_attn_wk, 8),
          "wv": t_slab(L0_attn_wv, 8), "wo": t_slab(L0_attn_wo, 8)}
    ws.update({k: np.ascontiguousarray(np.asarray(v, dtype=np.float32))
               for k, v in (("wg", L0_ffn_w_gate), ("wu", L0_ffn_w_up),
                            ("wd", L0_ffn_w_down))})
    idb = np.eye(128, dtype=ml_dtypes.bfloat16)
    nc = build_nc(D_, I3_, NTOK, n_group)
    in_maps = []
    for core in range(2 * n_group):
        b, cp = core // n_group, core % n_group
        xq = np.ascontiguousarray(x[b, NTOK * cp:NTOK * (cp + 1)])
        xf = np.ascontiguousarray(
            x[b].reshape(D_, NVT, D_)[:, NV * cp:NV * (cp + 1), :]
            .reshape(NTOK, D_))
        m = {"xq": xq, "xf": xf, "idb": idb}
        m.update(ws)
        in_maps.append(m)
    res = run_bass_kernel_spmd(nc, in_maps, core_ids=list(range(2 * n_group)))
    out = np.empty((B, NSEQ, D_), dtype=np.float32)
    for core in range(2 * n_group):
        b, cp = core // n_group, core % n_group
        o = np.asarray(res.results[core]["out"])
        out[b].reshape(D_, NVT, D_)[:, NV * cp:NV * (cp + 1), :] = \
            o.reshape(D_, NV, D_)
    return out
